# version 15
# baseline (speedup 1.0000x reference)
import math
import contextlib
import numpy as np
import ml_dtypes
import concourse.bass as bass
import concourse.mybir as mybir
from concourse.bass_utils import run_bass_kernel_spmd

F32 = mybir.dt.float32
F32R = mybir.dt.float32r
BF16 = mybir.dt.bfloat16
AF = mybir.ActivationFunctionType
ALU = mybir.AluOpType

D = 2048
TOK = 2048
NKC = D // 128
H = 16
NCH = TOK // 128
FH = 8192
FHC = FH // 128
QKVO = 3 * D
EPS = 1e-5
HSCALE = 1.0 / math.sqrt(FH)


def _slopes(n_heads):
    start = 2.0 ** (-2.0 ** (-(math.log2(n_heads) - 3)))
    return np.array([start ** (i + 1) for i in range(n_heads)], dtype=np.float64)


LAM = np.exp(-_slopes(H))
LAM_C = LAM ** 128
LAM_CORE = LAM ** 2048

CB_ID = 0
CB_H128 = 128
CB_H64 = 256
CB_MASK = 320
CB_DQ = CB_MASK + H * 128
CB_COLS = CB_DQ + H * 128
CF_ID = 0
CF_ONES = 128
CF_DKG = 256
CF_DKL = 512
CF_H64F = 528
CF_LAMC = 592
CF_COLS = 608


def _sylvester(n):
    Hm = np.array([[1.0]])
    while Hm.shape[0] < n:
        Hm = np.block([[Hm, Hm], [Hm, -Hm]])
    return Hm


def host_consts():
    cb = np.zeros((128, CB_COLS), dtype=np.float64)
    cb[:, CB_ID:CB_ID + 128] = np.eye(128)
    cb[:, CB_H128:CB_H128 + 128] = _sylvester(128)
    cb[0:64, CB_H64:CB_H64 + 64] = _sylvester(64)
    pos = np.arange(128, dtype=np.float64)
    for h in range(H):
        diff = pos[None, :] - pos[:, None]
        cb[:, CB_MASK + 128 * h:CB_MASK + 128 * (h + 1)] = np.where(
            diff >= 0, LAM[h] ** np.maximum(diff, 0.0), 0.0)
        cb[:, CB_DQ + 128 * h:CB_DQ + 128 * (h + 1)] = (LAM[h] ** (pos + 1.0))[None, :]
    cf = np.zeros((128, CF_COLS), dtype=np.float64)
    cf[:, CF_ID:CF_ID + 128] = np.eye(128)
    cf[:, CF_ONES:CF_ONES + 128] = 1.0
    u = np.arange(128, dtype=np.float64)
    for h in range(H):
        for j in range(NCH):
            cf[:, CF_DKG + h * 16 + j] = LAM[h] ** (2047.0 - 128.0 * j - u)
        cf[:, CF_DKL + h] = LAM[h] ** (127.0 - u)
    cf[0:64, CF_H64F:CF_H64F + 64] = _sylvester(64)
    for h in range(H):
        cf[:, CF_LAMC + h] = LAM_C[h]
    csins = []
    for r in range(8):
        rank = r % 4
        cs = np.zeros((128, 64), dtype=np.float64)
        for rp in range(rank):
            for h in range(H):
                cs[:, rp * 16 + h] = LAM_CORE[h] ** (rank - 1 - rp)
        csins.append(cs.astype(np.float32))
    return cb.astype(ml_dtypes.bfloat16), cf.astype(np.float32), csins


class Sem:
    __slots__ = ("h", "n", "name")

    def __init__(self, nc, stack, name):
        self.h = stack.enter_context(nc.semaphore(name))
        self.n = 0
        self.name = name


class Em:
    ENGS = ("sync", "gpsimd", "tensor", "vector", "scalar")

    def __init__(self, nc):
        self.nc = nc
        self.q = {e: [] for e in self.ENGS}

    def raw(self, eng, fn):
        self.q[eng].append(fn)

    def op(self, eng, make, waits=(), incs=()):
        wb = tuple((s.h, int(v)) for (s, v) in waits if int(v) > 0)
        ib = []
        for (s, k) in incs:
            s.n += k
            ib.append((s.h, k))
        assert len(ib) <= 1, "hardware allows one semaphore update per instruction"
        ib = tuple(ib)

        def f(e, make=make, wb=wb, ib=ib):
            for hh, v in wb:
                e.wait_ge(hh, v)
            inst = make(e)
            for hh, k in ib:
                inst.then_inc(hh, k)
        self.q[eng].append(f)

    def dma(self, eng, out, in_, sem, waits=(), **kw):
        self.op(eng, lambda e, out=out, in_=in_, kw=kw: e.dma_start(out=out, in_=in_, **kw),
                waits=waits, incs=((sem, 16),))
        return sem.n

    def wait(self, eng, sem, val):
        self.op(eng, lambda e: e.nop(), waits=((sem, val),))

    def flush(self, block):
        for name in self.ENGS:
            fns = self.q[name]
            getattr(block, name)(lambda e, fns=fns: [f(e) for f in fns])


class Bank:

    def __init__(self, nc, stack, em, i):
        self.ap = stack.enter_context(nc.psum_tensor(f"bank{i}", [128, 512], F32))
        self.s_mm = Sem(nc, stack, f"b{i}_mm")
        self.s_ev = Sem(nc, stack, f"b{i}_ev")
        self.em = em

    @property
    def bf(self):
        return self.ap.bitcast(BF16)

    def mms(self, makes, extra_waits=(), extra_incs=()):
        n = len(makes)
        for i, mk in enumerate(makes):
            w = (((self.s_ev, self.s_ev.n),) + tuple(extra_waits)) if i == 0 else ()
            inc = (((self.s_mm, 1),) + tuple(extra_incs)) if i == n - 1 else ()
            self.em.op("tensor", mk, waits=w, incs=inc)

    def ev(self, eng, make, extra_waits=(), extra_incs=()):
        self.em.op(eng, make,
                   waits=((self.s_mm, self.s_mm.n),) + tuple(extra_waits),
                   incs=((self.s_ev, 1),) + tuple(extra_incs))


def build(replica_groups, debug=False):
    nc = bass.Bass("TRN2", target_bir_lowering=False)
    group = len(replica_groups[0])

    x = nc.dram_tensor("x", [TOK, D], F32, kind="ExternalInput").ap()
    w_qkv = nc.dram_tensor("w_qkv", [QKVO, D], F32, kind="ExternalInput").ap()
    b_qkv = nc.dram_tensor("b_qkv", [QKVO], F32, kind="ExternalInput").ap()
    w_out = nc.dram_tensor("w_out", [D, D], F32, kind="ExternalInput").ap()
    b_out = nc.dram_tensor("b_out", [D], F32, kind="ExternalInput").ap()
    W1 = nc.dram_tensor("W1", [FH, D], F32, kind="ExternalInput").ap()
    W2 = nc.dram_tensor("W2", [D, FH], F32, kind="ExternalInput").ap()
    W3 = nc.dram_tensor("W3", [FH, D], F32, kind="ExternalInput").ap()
    cb_d = nc.dram_tensor("cbE", [128, CB_COLS], BF16, kind="ExternalInput").ap()
    cf_d = nc.dram_tensor("cf", [128, CF_COLS], F32, kind="ExternalInput").ap()
    csin_d = nc.dram_tensor("csin", [128, 64], F32, kind="ExternalInput").ap()
    out = nc.dram_tensor("out", [TOK, D], F32, kind="ExternalOutput").ap()

    x_fmd = nc.dram_tensor("x_fmd", [D, TOK], BF16).ap()
    q_bfd = nc.dram_tensor("q_bfd", [D, TOK], BF16).ap()
    k_bfd = nc.dram_tensor("k_bfd", [D, TOK], BF16).ap()
    y_bfd = nc.dram_tensor("y_bfd", [D, TOK], BF16).ap()
    wout_t = nc.dram_tensor("wout_t", [D, D], BF16).ap()
    w1t = nc.dram_tensor("w1t", [FH, D], BF16).ap()
    w3t = nc.dram_tensor("w3t", [FH, D], BF16).ap()
    w2sb = nc.dram_tensor("w2sb", [D, FH], BF16).ap()
    w2p = nc.dram_tensor("w2p", [D, FH], BF16).ap()
    A_dram = nc.dram_tensor("A_dram", [H * 128, 128], F32).ap()
    A_gath = nc.dram_tensor("A_gath", [group * H * 128, 128], F32).ap()

    dbg = {}
    if debug:
        for nm, shp, dt in (("dbg_qfm", [D, TOK], BF16), ("dbg_kfm", [D, TOK], BF16),
                            ("dbg_A", [H * 128, 128], F32), ("dbg_y", [D, TOK], BF16),
                            ("dbg_w2p", [D, FH], BF16), ("dbg_att", [128, 8192], BF16),
                            ("dbg_res", [128, 8192], F32),
                            ("dbg_rows", [4, 2048], F32), ("dbg_rows3", [8, 2048], F32)):
            dbg[nm] = nc.dram_tensor(nm, shp, dt, kind="ExternalOutput").ap()

    em = Em(nc)

    with contextlib.ExitStack() as st:
        E = st.enter_context

        cf = E(nc.sbuf_tensor("cf_sb", [128, CF_COLS], F32))
        csin = E(nc.sbuf_tensor("csin_sb", [128, 64], F32))
        ones_bf = E(nc.sbuf_tensor("ones_bf", [128, 128], BF16))
        bias_stage = E(nc.sbuf_tensor("bias_stage", [64, 128], F32))
        bias_sb = E(nc.sbuf_tensor("bias_sb", [128, 64], F32))
        bqkv = bias_sb[:, 0:48]
        bout = bias_sb[:, 48:64]

        B = [Bank(nc, st, em, i) for i in range(8)]

        s_ld = Sem(nc, st, "s_ld")
        s_wl = [Sem(nc, st, "s_wl0"), Sem(nc, st, "s_wl1")]
        s_ps = [Sem(nc, st, "s_ps0"), Sem(nc, st, "s_ps1")]
        s_st = Sem(nc, st, "s_st")
        s_wpp = [Sem(nc, st, "s_wpp0"), Sem(nc, st, "s_wpp1")]
        s_out = Sem(nc, st, "s_out")
        s_cc = Sem(nc, st, "s_cc")
        s_ev = Sem(nc, st, "s_ev")
        s_misc = Sem(nc, st, "s_misc")
        s_xfm = Sem(nc, st, "s_xfm")
        s_xs = Sem(nc, st, "s_xs")
        s_dbg = Sem(nc, st, "s_dbg")
        s_pld = [Sem(nc, st, f"s_pld{i}") for i in range(4)]
        s_pst4 = [Sem(nc, st, f"s_pst{i}") for i in range(4)]
        s_gl = Sem(nc, st, "s_gl")
        s_q = Sem(nc, st, "s_q")
        s_kq = [Sem(nc, st, "s_kq0"), Sem(nc, st, "s_kq1")]
        s_kx = Sem(nc, st, "s_kx")
        s_wop = [Sem(nc, st, "s_wop0"), Sem(nc, st, "s_wop1")]
        s_y = Sem(nc, st, "s_y")
        s_sbf = Sem(nc, st, "s_sbf")
        s_w1p = [Sem(nc, st, "s_w1p0"), Sem(nc, st, "s_w1p1")]
        s_w3 = Sem(nc, st, "s_w3")
        s_w2p = [Sem(nc, st, "s_w2p0"), Sem(nc, st, "s_w2p1")]
        s_yg = Sem(nc, st, "s_yg")
        s_b_ld = [Sem(nc, st, "s_b_ld0"), Sem(nc, st, "s_b_ld1")]
        s_b_st = [Sem(nc, st, "s_b_st0"), Sem(nc, st, "s_b_st1")]
        s_a_ld = [Sem(nc, st, "s_a_ld0"), Sem(nc, st, "s_a_ld1")]
        s_a_st = [Sem(nc, st, "s_a_st0"), Sem(nc, st, "s_a_st1")]

        em.dma("sync", cf[:, :], cf_d, s_ld)
        em.dma("sync", csin[:, :], csin_d, s_ld)
        em.dma("sync", bias_stage[0:48, :], b_qkv.rearrange("(c p) -> c p", p=128), s_ld)
        em.dma("sync", bias_stage[48:64, :], b_out.rearrange("(c p) -> c p", p=128), s_ld)

        with contextlib.ExitStack() as stE:
            EE = stE.enter_context
            cb1 = EE(nc.sbuf_tensor("cb1", [128, 192], BF16))
            v_tm = EE(nc.sbuf_tensor("v_tm", [128, NCH * H * 128], BF16))

            em.dma("sync", cb1[:, 0:128], cb_d[:, CB_ID:CB_ID + 128], s_ld)
            em.dma("sync", cb1[:, 128:192], cb_d[:, CB_H64:CB_H64 + 64], s_ld)
            ld_all = s_ld.n
            em.op("vector", lambda e: e.memset(ones_bf[:, :], 1.0),
                  waits=((s_ld, ld_all),), incs=((s_ev, 1),))
            B[7].mms([lambda e, o_=B[7].ap[:, 0:64], i_=bias_stage[0:64, :],
                      id_=cf[0:64, CF_ID:CF_ID + 64]: e.transpose(o_, i_, id_)],
                     extra_waits=((s_ld, ld_all),))
            B[7].ev("vector", lambda e, d=bias_sb[:, :], s=B[7].ap:
                    e.tensor_copy(d, s[:, 0:64]))
            bias_ready = (B[7].s_ev, B[7].s_ev.n)
            ID_BF = cb1[:, 0:128]
            cb1H64 = cb1[0:64, 128:192]
            ID_F = cf[:, CF_ID:CF_ID + 128]
            ONES_COL_F = cf[:, CF_ONES:CF_ONES + 1]
            ONES_ROW_F = cf[0:1, CF_ONES:CF_ONES + 128]

            trpool = [B[4], B[5], B[6]]
            _tr = [0]

            def next_tr():
                _tr[0] = (_tr[0] + 1) % 3
                return trpool[_tr[0]]

            wst_free = {0: (), 1: ()}
            _wslot = [0]

            def wtr_slice(srcW, r0, dst4_fn, first_evac_waits=(), src_cols=D):
                par = _wslot[0] % 2
                _wslot[0] += 1
                stg = wstage[:, par * D:par * D + src_cols]
                em.dma("sync", stg, srcW[r0:r0 + 128, :], s_wl[par],
                       waits=wst_free[par])
                ld = (s_wl[par], s_wl[par].n)
                for q in range(src_cols // 512):
                    tb = next_tr()
                    makes = []
                    for i4 in range(4):
                        kc = q * 4 + i4
                        makes.append(
                            lambda e,
                            o_=tb.ap.bitcast(F32)[:, i4 * 128:(i4 + 1) * 128],
                            i_=stg[:, kc * 128:(kc + 1) * 128], id_=ID_F:
                            e.transpose(o_, i_, id_))
                    tb.mms(makes, extra_waits=(ld,) if q == 0 else ())
                    exw = first_evac_waits if q == 0 else ()
                    tb.ev("vector", lambda e, d=dst4_fn(q),
                          s=tb.ap.bitcast(F32)[:, 0:512]
                          .rearrange("p (a b) -> p a b", a=4):
                          e.tensor_copy(d, s),
                          extra_waits=exw)
                wst_free[par] = tuple((bk.s_mm, bk.s_mm.n) for bk in trpool)
                return tuple((bk.s_ev, bk.s_ev.n) for bk in trpool)

            prep_slices = []
            _pslot = [0]

            _pidx = [0]

            def mk_prep(srcW, dst_t, ROWS):
                for oc in range(ROWS // 128):
                    ppar = _pidx[0] % 2
                    _pidx[0] += 1

                    def slice_fn(oc=oc, srcW=srcW, dst_t=dst_t, ppar=ppar):
                        pst = prepstage[:, ppar * D:(ppar + 1) * D]
                        caps = wtr_slice(
                            srcW, oc * 128,
                            lambda q, pst=pst: pst[:, q * 512:(q + 1) * 512]
                            .rearrange("p (a b) -> p a b", a=4),
                            first_evac_waits=((s_ps[ppar], s_ps[ppar].n),))
                        em.dma("gpsimd", dst_t[oc * 128:(oc + 1) * 128, :],
                               pst[:, :], s_ps[ppar], waits=caps)
                    prep_slices.append(slice_fn)

            mk_prep(w_out, wout_t, D)
            prep_wout_done = ((s_ps[0], (_pidx[0] + 1) // 2 * 16),
                              (s_ps[1], _pidx[0] // 2 * 16))
            mk_prep(W1, w1t, FH)
            mk_prep(W3, w3t, FH)
            prep_w13_done = ((s_ps[0], (_pidx[0] + 1) // 2 * 16),
                             (s_ps[1], _pidx[0] // 2 * 16))

            w2view = W2.rearrange("d (h1 h0) -> h1 d h0", h1=64)
            w2sbv = w2sb.rearrange("d (h1 h0) -> h1 d h0", h1=64)

            def b_slice(ds, ppar):
                par = _wslot[0] % 2
                _wslot[0] += 1
                pst = prepstage[:, ppar * D:(ppar + 1) * D]
                stg = wstage[0:64, par * D:(par + 1) * D]
                em.dma("sync", stg.rearrange("p (a b) -> p a b", a=16),
                       w2view[:, ds * 16:(ds + 1) * 16, :], s_wl[par],
                       waits=wst_free[par])
                ld = (s_wl[par], s_wl[par].n)
                em.op("scalar", lambda e, d=pst[0:64, :], s=stg:
                      e.activation(d, s, AF.Copy),
                      waits=(ld, (s_ps[ppar], s_ps[ppar].n)), incs=((s_misc, 1),))
                cast_r = s_misc.n
                caps = []
                for q4 in range(4):
                    tb = next_tr()
                    tb.mms([lambda e, o_=tb.ap[0:64, :], l_=cb1H64,
                            r_=pst[0:64, q4 * 512:(q4 + 1) * 512]:
                            e.matmul(o_, l_, r_, start=True, stop=True)],
                           extra_waits=((s_misc, cast_r),) if q4 == 0 else ())
                    tb.ev("vector",
                          lambda e, d=pst[64:128, q4 * 512:(q4 + 1) * 512],
                          s=tb.ap: e.tensor_copy(d, s[0:64, 0:512]))
                    caps.append((tb.s_ev, tb.s_ev.n))
                em.dma("gpsimd", w2sbv[:, ds * 16:(ds + 1) * 16, :],
                       pst[64:128, :].rearrange("p (a b) -> p a b", a=16),
                       s_ps[ppar], waits=tuple(caps))
                wst_free[par] = tuple((bk.s_mm, bk.s_mm.n) for bk in trpool)

            for ds in range(128):
                ppar = _pidx[0] % 2
                _pidx[0] += 1
                prep_slices.append(lambda ds=ds, ppar=ppar: b_slice(ds, ppar))
            prep_b_done = ((s_ps[0], (_pidx[0] + 1) // 2 * 16),
                           (s_ps[1], _pidx[0] // 2 * 16))

            def pump(n):
                for _ in range(n):
                    if prep_slices:
                        prep_slices.pop(0)()

            A_sb = EE(nc.sbuf_tensor("A_sb", [128, H * 128], F32))
            stP1 = contextlib.ExitStack()
            E1 = stP1.enter_context
            wstage = E1(nc.sbuf_tensor("wstage", [128, 2 * D], F32))
            prepstage = E1(nc.sbuf_tensor("prepstage", [128, 2 * D], BF16))
            x_fm = E1(nc.sbuf_tensor("x_fm", [128, NKC * TOK], BF16))
            for tt in range(NCH):
                fw = ((s_ld, ld_all),) if tt == 0 else ()
                xfm_done = wtr_slice(
                    x, tt * 128,
                    lambda q, tt=tt:
                    x_fm.rearrange("p (kc t) -> p kc t", t=TOK)
                    [:, q * 4:(q + 1) * 4, tt * 128:tt * 128 + 128],
                    first_evac_waits=fw)
            for kc in range(NKC):
                em.dma("gpsimd", x_fmd[kc * 128:(kc + 1) * 128, :],
                       x_fm[:, kc * TOK:(kc + 1) * TOK], s_xs,
                       waits=xfm_done if kc == 0 else ())
            xfmd_done = s_xs.n

            wpan = [E1(nc.sbuf_tensor(f"wpan{i}", [128, NKC * 512], BF16)) for i in range(2)]
            pstage = [E1(nc.sbuf_tensor(f"pstage{i}", [128, TOK], BF16)) for i in range(2)]
            kdecT4 = E1(nc.sbuf_tensor("kdecT4", [128, 1024], BF16))
            v_tm4 = v_tm.rearrange("p (j x) -> p j x", x=H * 128)

            projB = [B[0], B[1], B[2], B[3]]
            AB = B[7]

            o_order = list(range(32, 48)) + list(range(16, 32)) + list(range(16))

            npan = len(o_order) // 4
            pan_ld = {}

            def load_panel(p, extra=()):
                buf = wpan[p % 2]
                caps = ()
                for oc4 in range(4):
                    o0 = o_order[4 * p + oc4] * 128
                    fw = tuple(extra) if oc4 == 0 else ()
                    caps = wtr_slice(
                        w_qkv, o0,
                        lambda q, oc4=oc4, buf=buf:
                        buf.rearrange("p (kc o) -> p kc o", o=512)
                        [:, q * 4:(q + 1) * 4, oc4 * 128:oc4 * 128 + 128],
                        first_evac_waits=fw)
                pan_ld[p] = caps

            load_panel(0)
            load_panel(1)

            pstage_free = {0: (), 1: ()}
            vtm_caps = ()

            for i_o, o in enumerate(o_order):
                p, par = i_o // 4, i_o % 2
                buf = wpan[p % 2]
                banks = [projB[t] for t in range(4)]
                for t in range(4):
                    makes = []
                    for kc in range(NKC):
                        lhsT = buf[:, kc * 512 + (i_o % 4) * 128:kc * 512 + (i_o % 4) * 128 + 128]
                        rhs = x_fm[:, kc * TOK + t * 512:kc * TOK + t * 512 + 512]
                        makes.append(lambda e, o_=banks[t].ap, l_=lhsT, r_=rhs,
                                     st_=(kc == 0), sp_=(kc == NKC - 1):
                                     e.matmul(o_[:, :], l_, r_, start=st_, stop=sp_))
                    exw = list(pan_ld[p])
                    if i_o == 0 and t == 0:
                        exw.extend(xfm_done)
                        exw.append((s_ld, ld_all))
                    banks[t].mms(makes, extra_waits=tuple(exw))
                if i_o % 4 == 3 and p + 2 < npan:
                    load_panel(p + 2, extra=((banks[3].s_mm, banks[3].s_mm.n),))
                for t in range(4):
                    exw = pstage_free[par] if t == 0 else ()
                    if i_o == 0 and t == 0:
                        exw = exw + ((s_ld, ld_all), bias_ready)
                    banks[t].ev("scalar",
                                lambda e, b=banks[t].ap,
                                d=pstage[par][:, t * 512:(t + 1) * 512],
                                bias=bqkv[:, o:o + 1]:
                                e.activation(d, b[:, :], AF.Identity, bias=bias, scale=1.0),
                                extra_waits=exw)
                my_pst = tuple((bk.s_ev, bk.s_ev.n) for bk in banks)

                typ = "v" if o >= 32 else ("k" if o >= 16 else "q")
                h = o - 32 if typ == "v" else (o - 16 if typ == "k" else o)

                if typ == "v":
                    for q4 in range(4):
                        tb = next_tr()
                        makes = []
                        for i4 in range(4):
                            j = q4 * 4 + i4
                            makes.append(lambda e, o_=tb.bf[:, i4 * 128:(i4 + 1) * 128],
                                         i_=pstage[par][:, j * 128:(j + 1) * 128],
                                         id_=ID_BF: e.transpose(o_, i_, id_))
                        tb.mms(makes, extra_waits=my_pst if q4 == 0 else ())
                        tb.ev("vector",
                              lambda e,
                              d=v_tm4[:, q4 * 4:(q4 + 1) * 4, h * 128:(h + 1) * 128],
                              s=tb.bf[:, 0:512].rearrange("p (a b) -> p a b", a=4):
                              e.tensor_copy(d, s))
                    pstage_free[par] = tuple((bk.s_mm, bk.s_mm.n) for bk in trpool)
                    if i_o == 15:
                        vtm_caps = tuple((bk.s_ev, bk.s_ev.n) for bk in trpool)
                elif typ == "k":
                    for q4 in range(4):
                        tb = next_tr()
                        makes = []
                        for i4 in range(4):
                            j = q4 * 4 + i4
                            makes.append(lambda e, o_=tb.bf[:, i4 * 128:(i4 + 1) * 128],
                                         i_=pstage[par][:, j * 128:(j + 1) * 128],
                                         id_=ID_BF: e.transpose(o_, i_, id_))
                        tb.mms(makes, extra_waits=my_pst if q4 == 0 else ())
                        half = (q4 % 2) * 512
                        tb.ev("vector",
                              lambda e,
                              d=kdecT4[:, half:half + 512]
                              .rearrange("p (a b) -> p a b", a=4),
                              s=tb.bf[:, 0:512].rearrange("p (a b) -> p a b", a=4),
                              m=cf[:, CF_DKG + h * 16 + q4 * 4:
                                   CF_DKG + h * 16 + q4 * 4 + 4]
                              .unsqueeze(2).broadcast_to([128, 4, 128]):
                              e.tensor_tensor(d, s, m, ALU.mult),
                              extra_waits=((AB.s_mm, AB.s_mm.n),))
                        exw = [(tb.s_ev, tb.s_ev.n)]
                        if q4 == 0:
                            exw.extend(vtm_caps)
                        for i4 in range(4):
                            j = q4 * 4 + i4
                            AB.mms([lambda e, o_=AB.ap[:, 0:128],
                                    l_=kdecT4[:, half + i4 * 128:half + (i4 + 1) * 128],
                                    r_=v_tm[:, (j * H + h) * 128:(j * H + h) * 128 + 128],
                                    st_=(j == 0), sp_=(j == NCH - 1):
                                    e.matmul(o_, l_, r_, start=st_, stop=sp_)],
                                   extra_waits=tuple(exw) if i4 == 0 else ())
                    AB.ev("vector", lambda e, d=A_sb[:, h * 128:(h + 1) * 128],
                          s=AB.ap: e.tensor_copy(d, s[:, 0:128]))
                    em.dma("gpsimd", k_bfd[h * 128:(h + 1) * 128, :], pstage[par][:, :],
                           s_kq[par], waits=my_pst)
                    pstage_free[par] = tuple((bk.s_mm, bk.s_mm.n) for bk in trpool) + \
                        ((s_kq[par], s_kq[par].n),)
                else:
                    em.dma("gpsimd", q_bfd[h * 128:(h + 1) * 128, :], pstage[par][:, :],
                           s_kq[par], waits=my_pst)
                    pstage_free[par] = ((s_kq[par], s_kq[par].n),)
                if o == 31:
                    A_done = (AB.s_ev, AB.s_ev.n)
                    em.dma("gpsimd", A_dram.rearrange("(h p) c -> p h c", p=128),
                           A_sb[:, :].rearrange("p (h c) -> p h c", h=H),
                           s_st, waits=(A_done,))
                    em.op("gpsimd", lambda e: e.collective_compute(
                        "AllGather", ALU.bypass, replica_groups=replica_groups,
                        ins=[A_dram.opt()], outs=[A_gath.opt()]),
                        waits=((s_st, s_st.n),), incs=((s_cc, 1),))
                pump(6)

            pump(len(prep_slices))
            kq_stores = ((s_kq[0], s_kq[0].n), (s_kq[1], s_kq[1].n))

            if debug:
                em.dma("gpsimd", dbg["dbg_A"], A_dram, s_dbg,
                       waits=((s_dbg, s_dbg.n),))
            stP1.close()
            for bk in B:
                em.wait("sync", bk.s_mm, bk.s_mm.n)
                em.wait("sync", bk.s_ev, bk.s_ev.n)
            em.wait("sync", s_xs, xfmd_done)
            em.wait("sync", s_kq[0], s_kq[0].n)
            em.wait("sync", s_kq[1], s_kq[1].n)
            em.wait("sync", s_st, s_st.n)
            em.wait("sync", s_wl[0], s_wl[0].n)
            em.wait("sync", s_wl[1], s_wl[1].n)
            em.wait("sync", s_ps[0], s_ps[0].n)
            em.wait("sync", s_ps[1], s_ps[1].n)

            with contextlib.ExitStack() as st2:
                E2 = st2.enter_context
                cb2 = E2(nc.sbuf_tensor("cb2", [128, CB_COLS - CB_MASK], BF16))
                em.dma("sync", cb2[:, :], cb_d[:, CB_MASK:CB_COLS], s_ld)
                cb2_ld = s_ld.n
                S = E2(nc.sbuf_tensor("S", [128, H * 128], F32))
                S_bf_all = E2(nc.sbuf_tensor("S_bf_all", [128, H * 128], BF16))
                q_pan = E2(nc.sbuf_tensor("q_pan", [128, NKC * 512], BF16))
                k_pan = E2(nc.sbuf_tensor("k_pan", [128, NKC * 512], BF16))
                attn_fm = E2(nc.sbuf_tensor("attn_fm", [128, NKC * 512], BF16))
                resid1 = E2(nc.sbuf_tensor("resid1", [128, NKC * 512], F32))
                woutb = [E2(nc.sbuf_tensor(f"woutb{i}", [128, NKC * 128], BF16))
                         for i in range(2)]
                sq_scr = E2(nc.sbuf_tensor("sq_scr", [128, 512], F32))
                rows = E2(nc.sbuf_tensor("rows", [1, 4 * 512], F32))
                qdec_all = E2(nc.sbuf_tensor("qdec_all", [128, H * 128], BF16))
                smask_all = E2(nc.sbuf_tensor("smask_all", [128, H * 128], BF16))
                kd2_all = E2(nc.sbuf_tensor("kd2_all", [128, H * 128], BF16))
                q_pv = q_pan.rearrange("p (h t) -> p h t", t=512)
                k_pv = k_pan.rearrange("p (h t) -> p h t", t=512)
                at_v = attn_fm.rearrange("p (h t) -> p h t", t=512)
                S_v = S.rearrange("p (h c) -> p h c", c=128)
                dq_v = cb2.rearrange("p (b c) -> p b c", c=128)
                cb2_m4 = cb2[:, 0:H * 128]

                for g in range(group):
                    em.dma("sync",
                           resid1[:, g * 2048:(g + 1) * 2048]
                           .rearrange("p (h c) -> p h c", h=H),
                           A_gath[g * 2048:(g + 1) * 2048, :]
                           .rearrange("(h p) c -> p h c", p=128),
                           s_gl, waits=((s_cc, s_cc.n),) if g == 0 else ())
                gl_done = s_gl.n
                for h in range(H):
                    em.op("vector", lambda e, d=S[:, h * 128:(h + 1) * 128],
                          a=resid1[:, h * 128:(h + 1) * 128], sc=csin[:, h:h + 1]:
                          e.tensor_scalar_mul(d, a, sc),
                          waits=((s_gl, gl_done),) if h == 0 else
                                ((s_misc, s_misc.n),),
                          incs=((s_misc, 1),))
                    for g in range(1, group):
                        em.op("vector", lambda e, d=S[:, h * 128:(h + 1) * 128],
                              a=resid1[:, (g * 2048 + h * 128):(g * 2048 + h * 128) + 128],
                              sc=csin[:, g * 16 + h:g * 16 + h + 1]:
                              e.scalar_tensor_tensor(d, a, sc, d, ALU.mult, ALU.add),
                              waits=((s_misc, s_misc.n),), incs=((s_misc, 1),))
                S_ready = (s_misc, s_misc.n)

                SA = [B[0], B[1], B[2], B[3]]
                KX = [B[4], B[5], B[6], B[7]]
                opB = [B[0], B[1]]
                stB = [B[2], B[3]]
                bcB = [B[4], B[5]]

                supd_cap = ()
                rows_dbg_gate = ()
                attn_free = ()
                ktr_caps = ()
                kpan_free = ()
                resid_free = ()

                qdec_last = 0
                for g2 in range(4):
                    t0 = g2 * 512
                    w = list(kq_stores) if g2 == 0 else \
                        list(mixer_done) + [(s_misc, qdec_last)]
                    em.dma("sync", q_pan[:, :],
                           q_bfd.rearrange("(c p) t -> p c t", p=128)[:, :, t0:t0 + 512],
                           s_q, waits=tuple(w))
                    em.dma("sync", k_pan[:, :],
                           k_bfd.rearrange("(c p) t -> p c t", p=128)[:, :, t0:t0 + 512],
                           s_kx, waits=kpan_free)
                    qk_done = ((s_q, s_q.n), (s_kx, s_kx.n))

                    for jj in range(4):
                        j = g2 * 4 + jj
                        first = (jj == 0)
                        w_s = list(supd_cap)
                        w_s += [(bk.s_mm, bk.s_mm.n) for bk in SA]
                        if g2 == 0 and first:
                            w_s += [S_ready, (s_ld, cb2_ld)]
                        em.op("scalar", lambda e, d=S_bf_all[:, :], s=S[:, :]:
                              e.copy(d, s), waits=tuple(w_s), incs=((s_sbf, 1),))
                        sbf_ready = s_sbf.n
                        em.op("vector", lambda e,
                              d=qdec_all.rearrange("p (h c) -> p h c", c=128),
                              a=q_pv[:, :, jj * 128:jj * 128 + 128],
                              m=dq_v[:, H:2 * H, :]:
                              e.tensor_tensor(d, a, m, ALU.mult),
                              waits=(qk_done if first else ()) +
                                    tuple((bk.s_mm, bk.s_mm.n) for bk in SA) +
                                    (((s_ld, cb2_ld),)
                                     if (g2 == 0 and first) else ()),
                              incs=((s_misc, 1),))
                        qdec_ready = qdec_last = s_misc.n
                        for b4 in range(4):
                            kb = KX[b4]
                            makes = []
                            for i4 in range(4):
                                hh = 4 * b4 + i4
                                makes.append(lambda e,
                                             o_=kb.bf[:, i4 * 128:(i4 + 1) * 128],
                                             i_=k_pv[:, hh, jj * 128:jj * 128 + 128],
                                             id_=ID_BF: e.transpose(o_, i_, id_))
                            kb.mms(makes, extra_waits=qk_done
                                   if (first and b4 == 0) else ())
                            kb.ev("vector", lambda e,
                                  d=kd2_all[:, b4 * 512:(b4 + 1) * 512]
                                  .rearrange("p (a b) -> p a b", a=4),
                                  s=kb.bf[:, 0:512].rearrange("p (a b) -> p a b", a=4),
                                  m=cf[:, CF_DKL + 4 * b4:CF_DKL + 4 * b4 + 4]
                                  .unsqueeze(2).broadcast_to([128, 4, 128]):
                                  e.tensor_tensor(d, s, m, ALU.mult))
                        for b4 in range(4):
                            kb = KX[b4]
                            makes = []
                            for i4 in range(4):
                                hh = 4 * b4 + i4
                                makes.append(lambda e,
                                             o_=kb.ap[:, i4 * 128:(i4 + 1) * 128],
                                             l_=kd2_all[:, hh * 128:(hh + 1) * 128],
                                             r_=v_tm[:, (j * H + hh) * 128:
                                                     (j * H + hh) * 128 + 128]:
                                             e.matmul(o_, l_, r_, start=True, stop=True))
                            kb.mms(makes)
                            em.op("vector", lambda e,
                                  d=S_v[:, 4 * b4:4 * b4 + 4, :],
                                  m=cf[:, CF_LAMC + 4 * b4:CF_LAMC + 4 * b4 + 4]
                                  .unsqueeze(2).broadcast_to([128, 4, 128]):
                                  e.tensor_tensor(d, d, m, ALU.mult),
                                  waits=((s_sbf, sbf_ready),) if b4 == 0 else (),
                                  incs=((s_misc, 1),))
                            kb.ev("vector", lambda e,
                                  d=S_v[:, 4 * b4:4 * b4 + 4, :],
                                  s=kb.ap[:, 0:512].rearrange("p (a b) -> p a b", a=4):
                                  e.tensor_tensor(d, d, s, ALU.add),
                                  extra_waits=((s_misc, s_misc.n),))
                        supd_cap = tuple((bk.s_ev, bk.s_ev.n) for bk in KX)
                        for b4 in range(4):
                            sb = SA[b4]
                            makes = []
                            for i4 in range(4):
                                hh = 4 * b4 + i4
                                makes.append(lambda e,
                                             o_=sb.ap[:, i4 * 128:(i4 + 1) * 128],
                                             l_=k_pv[:, hh, jj * 128:jj * 128 + 128],
                                             r_=q_pv[:, hh, jj * 128:jj * 128 + 128]:
                                             e.matmul(o_, l_, r_, start=True, stop=True))
                            sb.mms(makes, extra_waits=qk_done
                                   if (first and b4 == 0) else ())
                            sb.ev("vector", lambda e,
                                  d=smask_all[:, b4 * 512:(b4 + 1) * 512],
                                  s=sb.ap[:, 0:512],
                                  m=cb2_m4[:, b4 * 512:(b4 + 1) * 512]:
                                  e.tensor_tensor(d, s, m, ALU.mult))
                        for b4 in range(4):
                            ab = SA[b4]
                            makes = []
                            for i4 in range(4):
                                hh = 4 * b4 + i4
                                makes.append(lambda e,
                                             o_=ab.ap[:, i4 * 128:(i4 + 1) * 128],
                                             l_=v_tm[:, (j * H + hh) * 128:
                                                     (j * H + hh) * 128 + 128],
                                             r_=smask_all[:, hh * 128:(hh + 1) * 128]:
                                             e.matmul(o_, l_, r_, start=True, stop=False))
                                makes.append(lambda e,
                                             o_=ab.ap[:, i4 * 128:(i4 + 1) * 128],
                                             l_=S_bf_all[:, hh * 128:(hh + 1) * 128],
                                             r_=qdec_all[:, hh * 128:(hh + 1) * 128]:
                                             e.matmul(o_, l_, r_, start=False, stop=True))
                            ab.mms(makes, extra_waits=((s_sbf, sbf_ready),
                                                       (s_misc, qdec_ready)))
                            exw = attn_free if (first and b4 == 0) else ()
                            ab.ev("scalar", lambda e,
                                  d=at_v[:, 4 * b4:4 * b4 + 4,
                                         jj * 128:jj * 128 + 128],
                                  s=ab.ap[:, 0:512].rearrange("p (a b) -> p a b", a=4):
                                  e.copy(d, s), extra_waits=exw)
                    mixer_done = tuple((bk.s_ev, bk.s_ev.n) for bk in SA)
                    ktr_caps = tuple((bk.s_mm, bk.s_mm.n) for bk in KX)

                    em.dma("sync", k_pan[:, :],
                           x_fmd.rearrange("(c p) t -> p c t", p=128)[:, :, t0:t0 + 512],
                           s_kx, waits=mixer_done + ((s_xs, xfmd_done),) + ktr_caps)
                    xp_done = (s_kx, s_kx.n)
                    r_readys = {}
                    for dc in range(NKC):
                        wb = woutb[dc % 2]
                        w = list(prep_wout_done) if (g2 == 0 and dc < 2) else []
                        if g2 > 0 or dc >= 2:
                            w.append((opB[dc % 2].s_mm, opB[dc % 2].s_mm.n))
                        em.dma("sync", wb[:, :],
                               wout_t[dc * 128:(dc + 1) * 128, :], s_wop[dc % 2],
                               waits=tuple(w))
                        wo_ready = (s_wop[dc % 2], s_wop[dc % 2].n)
                        ob = opB[dc % 2]
                        makes = []
                        for kc in range(NKC):
                            makes.append(lambda e, o_=ob.ap,
                                         l_=wb[:, kc * 128:(kc + 1) * 128],
                                         r_=attn_fm[:, kc * 512:(kc + 1) * 512],
                                         st_=(kc == 0), sp_=(kc == NKC - 1):
                                         e.matmul(o_[:, :], l_, r_, start=st_, stop=sp_))
                        ob.mms(makes, extra_waits=(wo_ready,) + mixer_done)
                        exw = ()
                        if dc == 0:
                            exw = ((stB[0].s_mm, stB[0].s_mm.n),
                                   (stB[1].s_mm, stB[1].s_mm.n)) + resid_free
                        ob.ev("scalar", lambda e, d=resid1[:, dc * 512:(dc + 1) * 512],
                              s=ob.ap, bias=bout[:, dc:dc + 1]:
                              e.activation(d, s[:, :], AF.Identity, bias=bias, scale=1.0),
                              extra_waits=exw)
                        em.op("vector", lambda e, d=resid1[:, dc * 512:(dc + 1) * 512],
                              a=k_pan[:, dc * 512:(dc + 1) * 512]:
                              e.tensor_tensor(d, d, a, ALU.add),
                              waits=(xp_done, (ob.s_ev, ob.s_ev.n)),
                              incs=((s_misc, 1),))
                        r_ready = s_misc.n
                        r_readys[dc] = r_ready
                        em.op("scalar", lambda e, d=sq_scr[:, :],
                              s=resid1[:, dc * 512:(dc + 1) * 512]:
                              e.activation(d, s, AF.Square),
                              waits=((s_misc, r_ready), (stB[1].s_mm, stB[1].s_mm.n)),
                              incs=((s_misc, 1),))
                        sq_ready = s_misc.n
                        stB[0].mms([lambda e, o_=stB[0].ap[0:1, :], l_=ONES_COL_F,
                                    r_=resid1[:, dc * 512:(dc + 1) * 512],
                                    st_=(dc == 0), sp_=(dc == NKC - 1):
                                    e.matmul(o_, l_, r_, start=st_, stop=sp_)],
                                   extra_waits=((s_misc, r_ready),))
                        stB[1].mms([lambda e, o_=stB[1].ap[0:1, :], l_=ONES_COL_F,
                                    r_=sq_scr[:, :],
                                    st_=(dc == 0), sp_=(dc == NKC - 1):
                                    e.matmul(o_, l_, r_, start=st_, stop=sp_)],
                                   extra_waits=((s_misc, sq_ready),))
                    attn_free = ((opB[0].s_mm, opB[0].s_mm.n), (opB[1].s_mm, opB[1].s_mm.n))
                    kpan_free = ((s_misc, s_misc.n),)
                    dbg_gate = ()
                    if debug and g2 == 0:
                        em.dma("gpsimd", dbg["dbg_att"], attn_fm[:, :], s_dbg,
                               waits=mixer_done + ((s_dbg, s_dbg.n),))
                        em.dma("gpsimd", dbg["dbg_res"], resid1[:, :], s_dbg,
                               waits=((s_misc, r_readys[NKC - 1]),
                                      (s_dbg, s_dbg.n)))
                        dbg_gate = ((s_dbg, s_dbg.n),)

                    Rm, Rv = rows[0:1, 0:512], rows[0:1, 512:1024]
                    Ra, Rb = rows[0:1, 1024:1536], rows[0:1, 1536:2048]
                    stB[0].ev("vector", lambda e, d=Rm, s=stB[0].ap:
                              e.tensor_scalar_mul(d, s[0:1, :], 1.0 / D),
                              extra_waits=((s_misc, s_misc.n),) + rows_dbg_gate)
                    stB[1].ev("vector", lambda e, d=Rv, s=stB[1].ap:
                              e.tensor_scalar_mul(d, s[0:1, :], 1.0 / D))
                    em.op("vector", lambda e, d=Ra, m=Rm: e.tensor_tensor(d, m, m, ALU.mult),
                          waits=((stB[0].s_ev, stB[0].s_ev.n),), incs=((s_misc, 1),))
                    em.op("vector", lambda e, d=Ra: e.tensor_scalar_add(d, d, -EPS),
                          waits=((s_misc, s_misc.n),), incs=((s_misc, 1),))
                    em.op("vector", lambda e, d=Rv, a=Ra:
                          e.tensor_tensor(d, d, a, ALU.subtract),
                          waits=((s_misc, s_misc.n), (stB[1].s_ev, stB[1].s_ev.n)),
                          incs=((s_misc, 1),))
                    var_ready = s_misc.n
                    em.op("scalar", lambda e, d=Ra, v=Rv:
                          e.activation(d, v, AF.Sqrt),
                          waits=((s_misc, var_ready),), incs=((s_misc, 1),))
                    sqrt_ready = s_misc.n
                    em.op("vector", lambda e, d=Ra: e.reciprocal(d, d),
                          waits=((s_misc, sqrt_ready),), incs=((s_misc, 1),))
                    em.op("vector", lambda e, d=Rb, m=Rm, a=Ra:
                          e.scalar_tensor_tensor(d, m, -1.0, a, ALU.mult, ALU.mult),
                          waits=((s_misc, s_misc.n),), incs=((s_misc, 1),))
                    ab_ready = s_misc.n
                    if debug:
                        em.dma("gpsimd", dbg["dbg_rows"][g2:g2 + 1, :], rows[0:1, :],
                               s_dbg, waits=((s_misc, ab_ready), (s_dbg, s_dbg.n)))
                        rows_dbg_gate = ((s_dbg, s_dbg.n),)
                    bcB[0].mms([lambda e, o_=bcB[0].ap, l_=ONES_ROW_F, r_=Ra:
                                e.matmul(o_[:, :], l_, r_, start=True, stop=True)],
                               extra_waits=((s_misc, ab_ready),))
                    bcB[1].mms([lambda e, o_=bcB[1].ap, l_=ONES_ROW_F, r_=Rb:
                                e.matmul(o_[:, :], l_, r_, start=True, stop=True)])
                    resid1_v = resid1.rearrange("p (dc t) -> p dc t", t=512)
                    attn_v3 = attn_fm.rearrange("p (dc t) -> p dc t", t=512)
                    em.op("vector", lambda e, d=resid1_v,
                          a=bcB[0].ap[:, :].unsqueeze(1).broadcast_to([128, NKC, 512]):
                          e.tensor_tensor(d, d, a, ALU.mult),
                          waits=((s_misc, r_readys[NKC - 1]),
                                 (bcB[0].s_mm, bcB[0].s_mm.n),
                                 (bcB[1].s_mm, bcB[1].s_mm.n)) + dbg_gate,
                          incs=((bcB[0].s_ev, 1),))
                    em.op("vector", lambda e, d=attn_v3, r=resid1_v,
                          b=bcB[1].ap[:, :].unsqueeze(1).broadcast_to([128, NKC, 512]):
                          e.tensor_tensor(d, r, b, ALU.add),
                          waits=((bcB[0].s_ev, bcB[0].s_ev.n),),
                          incs=((bcB[1].s_ev, 1),))
                    y_ready = (bcB[1].s_ev, bcB[1].s_ev.n)
                    resid_free = (y_ready,)
                    em.dma("gpsimd",
                           y_bfd.rearrange("(c p) t -> p c t", p=128)[:, :, t0:t0 + 512],
                           attn_fm[:, :], s_y, waits=(y_ready,))
                    attn_free = attn_free + ((s_y, s_y.n),)

                y_stores = s_y.n
                if debug:
                    em.dma("gpsimd", dbg["dbg_qfm"], q_bfd, s_dbg,
                           waits=((s_dbg, s_dbg.n),))
                    em.dma("gpsimd", dbg["dbg_kfm"], k_bfd, s_dbg,
                           waits=((s_dbg, s_dbg.n),))
                    em.dma("gpsimd", dbg["dbg_y"], y_bfd, s_dbg,
                           waits=((s_y, y_stores), (s_dbg, s_dbg.n)))

        with contextlib.ExitStack() as st3:
            E3 = st3.enter_context
            GROUPS = [(0, 768), (768, 768), (1536, 512)]
            GMAX = 768
            hidden = E3(nc.sbuf_tensor("hidden", [128, FHC * GMAX], BF16))
            y_grp = E3(nc.sbuf_tensor("y_grp", [128, NKC * GMAX], BF16))
            u1 = E3(nc.sbuf_tensor("u1", [128, 3 * NKC * 512], BF16))
            w1pan = u1[:, 0:NKC * 512]
            w1pre = u1[:, NKC * 512:2 * NKC * 512]
            w3pan = u1[:, 2 * NKC * 512:3 * NKC * 512]
            u_res2 = u1[:, 0:NKC * GMAX]
            sq3 = u1[:, 12288:12800]
            u1f = u1.bitcast(F32)
            apply2 = [u1f[:, 6400:6912], u1f[:, 6912:7424]]
            stag = u1f[:, 7424:9472]
            rows3 = u1f[0:1, 9472:11520]
            w2pan = [E3(nc.sbuf_tensor(f"w2pan{i}", [128, FHC * 128], BF16))
                     for i in range(2)]
            aSlot = [w2pan[0][:, 6144:8192], w2pan[1][:, 6144:8192]]
            cbE2 = E3(nc.sbuf_tensor("cbE2", [128, 1216], BF16))
            silu_scr = [cbE2[:, 192:704], cbE2[:, 704:1216]]

            for bk in B:
                em.wait("sync", bk.s_mm, bk.s_mm.n)
                em.wait("sync", bk.s_ev, bk.s_ev.n)
                em.wait("scalar", bk.s_mm, bk.s_mm.n)
                em.wait("scalar", bk.s_ev, bk.s_ev.n)
            em.wait("sync", s_misc, s_misc.n)
            em.wait("sync", s_sbf, s_sbf.n)
            em.wait("sync", s_y, s_y.n)
            em.wait("sync", s_st, s_st.n)
            em.wait("sync", s_dbg, s_dbg.n)
            em.wait("scalar", s_misc, s_misc.n)
            em.wait("scalar", s_sbf, s_sbf.n)
            em.wait("scalar", s_y, s_y.n)
            em.wait("scalar", s_st, s_st.n)
            em.wait("scalar", s_dbg, s_dbg.n)
            for s4 in s_pst4:
                em.wait("sync", s4, s4.n)
                em.wait("scalar", s4, s4.n)
            em.dma("sync", cbE2[:, 64:192], cb_d[:, CB_H128:CB_H128 + 128], s_ld)
            cbE2_ld = s_ld.n

            hbB = [B[0], B[1], B[2], B[3], B[4], B[5]]
            prepB = [B[6], B[7]]
            w2B = [B[0], B[1], B[2], B[3]]
            st3B = [B[4], B[5], B[6], B[7]]
            bc3B_of = {0: [B[4], B[5]], 1: [B[6], B[7]]}
            tr3B_of = {0: [B[0], B[1]], 1: [B[2], B[3]]}

            prep_pe = []

            def a_slice(hc):
                par = hc % 2
                buf = aSlot[par]
                w = list(prep_b_done) + [(s_ld, cbE2_ld)] if hc < 2 else \
                    [(s_a_st[par], s_a_st[par].n)]
                em.dma("sync", buf[:, :], w2sb[:, hc * 128:(hc + 1) * 128], s_a_ld[par],
                       waits=tuple(w), transpose=True)
                ld = s_a_ld[par].n
                pb = prepB[par]
                for q4 in range(4):
                    pb.mms([lambda e, o_=pb.ap, l_=cbE2[:, 64:192],
                            r_=buf[:, q4 * 512:(q4 + 1) * 512]:
                            e.matmul(o_[:, :], l_, r_, start=True, stop=True)],
                           extra_waits=((s_a_ld[par], ld),) if q4 == 0 else ())
                    pb.ev("scalar", lambda e, d=buf[:, q4 * 512:(q4 + 1) * 512],
                          s=pb.ap: e.activation(d, s[:, :], AF.Copy, scale=HSCALE))
                em.dma("scalar" if par == 0 else "gpsimd",
                       w2p.rearrange("(dc p) x -> p dc x", p=128)
                       [:, :, hc * 128:(hc + 1) * 128],
                       buf.rearrange("p (dc d) -> p dc d", d=128),
                       s_a_st[par], waits=((pb.s_ev, pb.s_ev.n),))

            for hc in range(FHC):
                prep_pe.append(lambda hc=hc: a_slice(hc))

            def pump3(n):
                for _ in range(n):
                    if prep_pe:
                        prep_pe.pop(0)()

            hid_free = ()
            yg_free = ()
            res2_free = ()
            w1_free = {0: (), 1: ()}
            w3_free = ()
            w2_free = {0: (), 1: ()}
            prev_hid = ()
            last_mult = {0: (), 1: ()}

            def hb_caps():
                return tuple((b_.s_ev, b_.s_ev.n) for b_ in hbB)

            for gi, (t0g, G) in enumerate(GROUPS):
                ttiles = [(0, 512), (512, G - 512)] if G > 512 else [(0, 512)]
                em.dma("sync", y_grp[:, 0:NKC * G].rearrange("p (c t) -> p c t", c=NKC),
                       y_bfd.rearrange("(c p) t -> p c t", p=128)[:, :, t0g:t0g + G],
                       s_yg, waits=((s_y, y_stores),) + yg_free if gi == 0 else yg_free)
                yg_done = s_yg.n

                first_hid = True
                for opan in range(FH // 512):
                    w1buf = w1pan if opan % 2 == 0 else w1pre
                    w = list(w1_free[opan % 2])
                    if gi == 0 and opan < 2:
                        w.extend(prep_w13_done)
                    if gi > 0 and opan == 0:
                        w.extend(prev_hid)
                    sem1 = s_w1p[opan % 2]
                    em.dma("sync", w1buf.rearrange("p (f c) -> p f c", c=D),
                           w1t.rearrange("(f p) c -> p f c", p=128)
                           [:, opan * 4:(opan + 1) * 4, :], sem1, waits=tuple(w))
                    w1_done = (sem1, sem1.n)
                    w3w = list(w3_free)
                    if gi == 0 and opan < 2:
                        w3w.extend(prep_w13_done)
                    em.dma("scalar", w3pan.rearrange("p (f c) -> p f c", c=D),
                           w3t.rearrange("(f p) c -> p f c", p=128)
                           [:, opan * 4:(opan + 1) * 4, :], s_w3, waits=tuple(w3w))
                    w3_done = (s_w3, s_w3.n)
                    gcaps, lcaps = [], []
                    for oc4 in range(4):
                        oc = opan * 4 + oc4
                        for ti, (tt, tw) in enumerate(ttiles):
                            gb = hbB[(oc * 4 + 2 * ti) % 6]
                            lb = hbB[(oc * 4 + 2 * ti + 1) % 6]
                            makes = []
                            for kc in range(NKC):
                                makes.append(lambda e, o_=gb.ap,
                                             l_=w1buf[:, oc4 * D + kc * 128:
                                                      oc4 * D + kc * 128 + 128],
                                             r_=y_grp[:, kc * G + tt:kc * G + tt + tw],
                                             st_=(kc == 0), sp_=(kc == NKC - 1), tw=tw:
                                             e.matmul(o_[:, 0:tw], l_, r_,
                                                      start=st_, stop=sp_))
                            exw = [w1_done]
                            if oc4 == 0 and ti == 0:
                                exw.append((s_yg, yg_done))
                            gb.mms(makes, extra_waits=tuple(exw))
                            makes = []
                            for kc in range(NKC):
                                makes.append(lambda e, o_=lb.ap,
                                             l_=w3pan[:, oc4 * D + kc * 128:
                                                      oc4 * D + kc * 128 + 128],
                                             r_=y_grp[:, kc * G + tt:kc * G + tt + tw],
                                             st_=(kc == 0), sp_=(kc == NKC - 1), tw=tw:
                                             e.matmul(o_[:, 0:tw], l_, r_,
                                                      start=st_, stop=sp_))
                            lb.mms(makes, extra_waits=(w3_done,))
                            gcaps.append((gb.s_mm, gb.s_mm.n))
                            lcaps.append((lb.s_mm, lb.s_mm.n))
                            sl = silu_scr[(oc * len(ttiles) + ti) % 2]
                            gb.ev("scalar", lambda e, d=sl[:, 0:tw], s=gb.ap, tw=tw:
                                  e.activation(d, s[:, 0:tw], AF.Silu),
                                  extra_waits=last_mult[(oc * len(ttiles) + ti) % 2])
                            sil_ready = (gb.s_ev, gb.s_ev.n)
                            exw = (sil_ready,)
                            if first_hid:
                                exw = exw + hid_free
                                first_hid = False
                            lb.ev("vector", lambda e,
                                  d=hidden[:, oc * G + tt:oc * G + tt + tw],
                                  a=sl[:, 0:tw], s=lb.ap, tw=tw:
                                  e.tensor_tensor(d, a, s[:, 0:tw], ALU.mult),
                                  extra_waits=exw)
                            last_mult[(oc * len(ttiles) + ti) % 2] = \
                                ((lb.s_ev, lb.s_ev.n),)
                    w1_free[opan % 2] = tuple(gcaps[-8:])
                    w3_free = tuple(lcaps[-8:])
                    if gi == 0:
                        pump3(12)
                hid_done = hb_caps()
                prev_hid = hid_done
                if gi == 0:
                    pump3(len(prep_pe))

                first_r2 = True
                stcapss = []
                r2caps = {}
                rows3_gate = globals().get('_nope', ())
                for dc in range(NKC):
                    wbuf = w2pan[dc % 2]
                    w = list(w2_free[dc % 2])
                    if gi == 0 and dc < 2:
                        w.extend([(s_a_st[0], s_a_st[0].n), (s_a_st[1], s_a_st[1].n)])
                    if dc < 2:
                        w.extend(hid_done)
                    semw2 = s_w2p[dc % 2]
                    em.dma("sync" if dc % 2 == 0 else "scalar", wbuf[:, :],
                           w2p[dc * 128:(dc + 1) * 128, :], semw2, waits=tuple(w))
                    w2_ready = (semw2, semw2.n)
                    mcaps = []
                    for ti, (tt, tw) in enumerate(ttiles):
                        mb = w2B[(dc * 2 + ti) % 4]
                        makes = []
                        for hc in range(FHC):
                            makes.append(lambda e, o_=mb.ap,
                                         l_=wbuf[:, hc * 128:(hc + 1) * 128],
                                         r_=hidden[:, hc * G + tt:hc * G + tt + tw],
                                         st_=(hc == 0), sp_=(hc == FHC - 1), tw=tw:
                                         e.matmul(o_[:, 0:tw], l_, r_,
                                                  start=st_, stop=sp_))
                        exw = [w2_ready]
                        if ti == 0 and dc == 0:
                            exw.extend(hid_done)
                        mb.mms(makes, extra_waits=tuple(exw))
                        mcaps.append((mb.s_mm, mb.s_mm.n))
                        exw2 = ()
                        if first_r2:
                            exw2 = res2_free
                            first_r2 = False
                        mb.ev("vector", lambda e,
                              d=u_res2[:, dc * G + tt:dc * G + tt + tw],
                              s=mb.ap, a=y_grp[:, dc * G + tt:dc * G + tt + tw], tw=tw:
                              e.tensor_tensor(d, s[:, 0:tw], a, ALU.add),
                              extra_waits=exw2)
                        r2_ready = (mb.s_ev, mb.s_ev.n)
                        r2caps[(ti, dc)] = r2_ready
                        em.op("scalar", lambda e, d=sq3[:, 0:tw],
                              s=u_res2[:, dc * G + tt:dc * G + tt + tw], tw=tw:
                              e.activation(d, s, AF.Square),
                              waits=(r2_ready,
                                     (st3B[2 * ti + 1].s_mm, st3B[2 * ti + 1].s_mm.n)),
                              incs=((s_misc, 1),))
                        sq_ready = s_misc.n
                        st3B[2 * ti].mms([lambda e, o_=st3B[2 * ti].ap[0:1, :],
                                          l_=ones_bf[:, 0:1],
                                          r_=u_res2[:, dc * G + tt:dc * G + tt + tw],
                                          st_=(dc == 0), sp_=(dc == NKC - 1), tw=tw:
                                          e.matmul(o_[0:1, 0:tw], l_, r_,
                                                   start=st_, stop=sp_)],
                                         extra_waits=(r2_ready,))
                        st3B[2 * ti + 1].mms([lambda e, o_=st3B[2 * ti + 1].ap[0:1, :],
                                              l_=ones_bf[:, 0:1], r_=sq3[:, 0:tw],
                                              st_=(dc == 0), sp_=(dc == NKC - 1), tw=tw:
                                              e.matmul(o_[0:1, 0:tw], l_, r_,
                                                       start=st_, stop=sp_)],
                                             extra_waits=((s_misc, sq_ready),))
                    w2_free[dc % 2] = tuple(mcaps)
                    stcapss = mcaps

                res2_caps = []
                for ti, (tt, tw) in enumerate(ttiles):
                    bc3B = bc3B_of[ti]
                    tr3B = tr3B_of[ti]
                    Rm = rows3[0:1, 0:tw]
                    Rv = rows3[0:1, 512:512 + tw]
                    Ra = rows3[0:1, 1024:1024 + tw]
                    Rb = rows3[0:1, 1536:1536 + tw]
                    st3B[2 * ti].ev("vector", lambda e, d=Rm, s=st3B[2 * ti].ap, tw=tw:
                                    e.tensor_scalar_mul(d, s[0:1, 0:tw], 1.0 / D),
                                    extra_waits=((s_misc, s_misc.n),) + rows3_gate)
                    st3B[2 * ti + 1].ev("vector", lambda e, d=Rv,
                                        s=st3B[2 * ti + 1].ap, tw=tw:
                                        e.tensor_scalar_mul(d, s[0:1, 0:tw], 1.0 / D))
                    em.op("vector", lambda e, d=Ra, m=Rm: e.tensor_tensor(d, m, m, ALU.mult),
                          waits=((st3B[2 * ti].s_ev, st3B[2 * ti].s_ev.n),),
                          incs=((s_misc, 1),))
                    em.op("vector", lambda e, d=Ra: e.tensor_scalar_add(d, d, -EPS),
                          waits=((s_misc, s_misc.n),), incs=((s_misc, 1),))
                    em.op("vector", lambda e, d=Rv, a=Ra:
                          e.tensor_tensor(d, d, a, ALU.subtract),
                          waits=((s_misc, s_misc.n),
                                 (st3B[2 * ti + 1].s_ev, st3B[2 * ti + 1].s_ev.n)),
                          incs=((s_misc, 1),))
                    var_r = s_misc.n
                    em.op("scalar", lambda e, d=Ra, v=Rv:
                          e.activation(d, v, AF.Sqrt),
                          waits=((s_misc, var_r),), incs=((s_misc, 1),))
                    sq_r = s_misc.n
                    em.op("vector", lambda e, d=Ra: e.reciprocal(d, d),
                          waits=((s_misc, sq_r),), incs=((s_misc, 1),))
                    em.op("vector", lambda e, d=Rb, m=Rm, a=Ra:
                          e.scalar_tensor_tensor(d, m, -1.0, a, ALU.mult, ALU.mult),
                          waits=((s_misc, s_misc.n),), incs=((s_misc, 1),))
                    ab_r = s_misc.n
                    if debug:
                        em.dma("gpsimd", dbg["dbg_rows3"][gi * 2 + ti:gi * 2 + ti + 1, :],
                               rows3[0:1, 0:2048], s_dbg,
                               waits=((s_misc, ab_r), (s_dbg, s_dbg.n)))
                        rows3_dbg = ((s_dbg, s_dbg.n),)
                    else:
                        rows3_dbg = ()
                    rows3_gate = rows3_dbg
                    bc3B[0].mms([lambda e, o_=bc3B[0].ap, l_=ONES_ROW_F, r_=Ra, tw=tw:
                                 e.matmul(o_[:, 0:tw], l_, r_, start=True, stop=True)],
                                extra_waits=((s_misc, ab_r),))
                    bc3B[1].mms([lambda e, o_=bc3B[1].ap, l_=ONES_ROW_F, r_=Rb, tw=tw:
                                 e.matmul(o_[:, 0:tw], l_, r_, start=True, stop=True)])
                    ntc = tw // 128
                    u_res2_v = u_res2[:, 0:NKC * G].rearrange("p (dc t) -> p dc t", t=G)
                    for tc in range(ntc):
                        for q in range(4):
                            par = (tc * 4 + q) % 2
                            scr = apply2[par]
                            w = [(s_misc, s_misc.n)]
                            w.append(r2caps[(ti, NKC - 1)])
                            if tc == 0 and q == 0:
                                w += [(bc3B[0].s_mm, bc3B[0].s_mm.n),
                                      (bc3B[1].s_mm, bc3B[1].s_mm.n)]
                            ptr = tr3B[(tc * 4 + q) % 2]
                            w.append((ptr.s_mm, ptr.s_mm.n))
                            last_q = (tc == ntc - 1 and q == 3)
                            em.op("vector", lambda e,
                                  d=scr.rearrange("p (a b) -> p a b", a=4),
                                  r=u_res2_v[:, 4 * q:4 * q + 4,
                                             tt + tc * 128:tt + tc * 128 + 128],
                                  a=bc3B[0].ap[:, tc * 128:(tc + 1) * 128]
                                  .unsqueeze(1).broadcast_to([128, 4, 128]):
                                  e.tensor_tensor(d, r, a, ALU.mult),
                                  waits=tuple(w),
                                  incs=((bc3B[0].s_ev, 1),) if last_q
                                  else ((s_misc, 1),))
                            mult_r = ((bc3B[0].s_ev, bc3B[0].s_ev.n) if last_q
                                      else (s_misc, s_misc.n))
                            em.op("vector", lambda e,
                                  d=scr.rearrange("p (a b) -> p a b", a=4),
                                  b=bc3B[1].ap[:, tc * 128:(tc + 1) * 128]
                                  .unsqueeze(1).broadcast_to([128, 4, 128]):
                                  e.tensor_tensor(d, d, b, ALU.add),
                                  waits=(mult_r,),
                                  incs=((bc3B[1].s_ev, 1),) if last_q
                                  else ((s_misc, 1),))
                            ap_r = ((bc3B[1].s_ev, bc3B[1].s_ev.n) if last_q
                                    else (s_misc, s_misc.n))
                            trb = tr3B[(tc * 4 + q) % 2]
                            makes = []
                            for i4 in range(4):
                                makes.append(lambda e,
                                             o_=trb.ap[:, i4 * 128:(i4 + 1) * 128],
                                             i_=scr[:, i4 * 128:(i4 + 1) * 128],
                                             id_=ID_F: e.transpose(o_, i_, id_))
                            trb.mms(makes, extra_waits=(ap_r,))
                            exw = ((s_out, s_out.n),) if q == 0 else ()
                            trb.ev("vector", lambda e,
                                   d=stag[:, q * 512:(q + 1) * 512],
                                   s=trb.ap: e.tensor_copy(d, s[:, 0:512]),
                                   extra_waits=exw)
                        stg_r = ((tr3B[0].s_ev, tr3B[0].s_ev.n),
                                 (tr3B[1].s_ev, tr3B[1].s_ev.n))
                        trow = t0g + tt + tc * 128
                        em.dma("gpsimd", out[trow:trow + 128, :], stag[:, :], s_out,
                               waits=stg_r)

                    res2_caps += [(st3B[2 * ti].s_mm, st3B[2 * ti].s_mm.n),
                                  (st3B[2 * ti + 1].s_mm, st3B[2 * ti + 1].s_mm.n)]

                hid_free = tuple(stcapss)
                yg_free = tuple(stcapss) + ((w2B[0].s_ev, w2B[0].s_ev.n),
                                            (w2B[1].s_ev, w2B[1].s_ev.n),
                                            (w2B[2].s_ev, w2B[2].s_ev.n),
                                            (w2B[3].s_ev, w2B[3].s_ev.n))
                res2_free = tuple(res2_caps) + ((s_misc, s_misc.n),)
                w13_safe = ((s_out, s_out.n), (s_misc, s_misc.n)) + \
                    tuple((bk.s_mm, bk.s_mm.n) for bk in B)
                if debug:
                    w13_safe = w13_safe + ((s_dbg, s_dbg.n),)
                w1_free = {0: w13_safe, 1: w13_safe}
                w3_free = w13_safe

            if debug:
                em.dma("gpsimd", dbg["dbg_w2p"], w2p, s_dbg,
                       waits=((s_a_st[0], s_a_st[0].n), (s_a_st[1], s_a_st[1].n),
                              (s_dbg, s_dbg.n)))

            em.wait("gpsimd", s_out, s_out.n)
            if debug:
                em.wait("gpsimd", s_dbg, s_dbg.n)

        block = st.enter_context(nc.Block())
        em.flush(block)

    return nc


def _shard_inputs(L0_attn_x, L0_attn_w_qkv, L0_attn_b_qkv, L0_attn_w_out,
                  L0_attn_b_out, L0_ffn_W1, L0_ffn_W2, L0_ffn_W3, n_cores=8):
    x = np.ascontiguousarray(np.asarray(L0_attn_x, dtype=np.float32))
    cb, cfv, csins = host_consts()
    shared = {
        "w_qkv": np.ascontiguousarray(L0_attn_w_qkv, dtype=np.float32),
        "b_qkv": np.ascontiguousarray(L0_attn_b_qkv, dtype=np.float32),
        "w_out": np.ascontiguousarray(L0_attn_w_out, dtype=np.float32),
        "b_out": np.ascontiguousarray(L0_attn_b_out, dtype=np.float32),
        "W1": np.ascontiguousarray(L0_ffn_W1, dtype=np.float32),
        "W2": np.ascontiguousarray(L0_ffn_W2, dtype=np.float32),
        "W3": np.ascontiguousarray(L0_ffn_W3, dtype=np.float32),
        "cbE": cb, "cf": cfv,
    }
    in_maps = []
    for r in range(n_cores):
        b, t0 = r // 4, 2048 * (r % 4)
        m = dict(shared)
        m["x"] = np.ascontiguousarray(x[b, t0:t0 + 2048, :])
        m["csin"] = csins[r]
        in_maps.append(m)
    return in_maps


def kernel(L0_attn_x, L0_attn_w_qkv, L0_attn_b_qkv, L0_attn_w_out, L0_attn_b_out,
           L0_ffn_W1, L0_ffn_W2, L0_ffn_W3):
    nc = build([[0, 1, 2, 3], [4, 5, 6, 7]])
    in_maps = _shard_inputs(L0_attn_x, L0_attn_w_qkv, L0_attn_b_qkv, L0_attn_w_out,
                            L0_attn_b_out, L0_ffn_W1, L0_ffn_W2, L0_ffn_W3)
    res = run_bass_kernel_spmd(nc, in_maps, core_ids=list(range(8)))
    outa = np.empty((2, 8192, 2048), dtype=np.float32)
    for r in range(8):
        b, t0 = r // 4, 2048 * (r % 4)
        outa[b, t0:t0 + 2048, :] = res.results[r]["out"]
    return outa


# version 16
# speedup vs baseline: 1.0336x; 1.0336x over previous
import math
import contextlib
import numpy as np
import ml_dtypes
import concourse.bass as bass
import concourse.mybir as mybir
from concourse.bass_utils import run_bass_kernel_spmd

F32 = mybir.dt.float32
F32R = mybir.dt.float32r
BF16 = mybir.dt.bfloat16
AF = mybir.ActivationFunctionType
ALU = mybir.AluOpType

D = 2048
TOK = 2048
NKC = D // 128
H = 16
NCH = TOK // 128
FH = 8192
FHC = FH // 128
QKVO = 3 * D
EPS = 1e-5
HSCALE = 1.0 / math.sqrt(FH)


def _slopes(n_heads):
    start = 2.0 ** (-2.0 ** (-(math.log2(n_heads) - 3)))
    return np.array([start ** (i + 1) for i in range(n_heads)], dtype=np.float64)


LAM = np.exp(-_slopes(H))
LAM_C = LAM ** 128
LAM_CORE = LAM ** 2048

CB_ID = 0
CB_H128 = 128
CB_H64 = 256
CB_MASK = 320
CB_DQ = CB_MASK + H * 128
CB_COLS = CB_DQ + H * 128
CF_ID = 0
CF_ONES = 128
CF_DKG = 256
CF_DKL = 512
CF_H64F = 528
CF_LAMC = 592
CF_COLS = 608


def _sylvester(n):
    Hm = np.array([[1.0]])
    while Hm.shape[0] < n:
        Hm = np.block([[Hm, Hm], [Hm, -Hm]])
    return Hm


def host_consts():
    cb = np.zeros((128, CB_COLS), dtype=np.float64)
    cb[:, CB_ID:CB_ID + 128] = np.eye(128)
    cb[:, CB_H128:CB_H128 + 128] = _sylvester(128)
    cb[0:64, CB_H64:CB_H64 + 64] = _sylvester(64)
    pos = np.arange(128, dtype=np.float64)
    for h in range(H):
        diff = pos[None, :] - pos[:, None]
        cb[:, CB_MASK + 128 * h:CB_MASK + 128 * (h + 1)] = np.where(
            diff >= 0, LAM[h] ** np.maximum(diff, 0.0), 0.0)
        cb[:, CB_DQ + 128 * h:CB_DQ + 128 * (h + 1)] = (LAM[h] ** (pos + 1.0))[None, :]
    cf = np.zeros((128, CF_COLS), dtype=np.float64)
    cf[:, CF_ID:CF_ID + 128] = np.eye(128)
    cf[:, CF_ONES:CF_ONES + 128] = 1.0
    u = np.arange(128, dtype=np.float64)
    for h in range(H):
        for j in range(NCH):
            cf[:, CF_DKG + h * 16 + j] = LAM[h] ** (2047.0 - 128.0 * j - u)
        cf[:, CF_DKL + h] = LAM[h] ** (127.0 - u)
    cf[0:64, CF_H64F:CF_H64F + 64] = _sylvester(64)
    for h in range(H):
        cf[:, CF_LAMC + h] = LAM_C[h]
    csins = []
    for r in range(8):
        rank = r % 4
        cs = np.zeros((128, 64), dtype=np.float64)
        for rp in range(rank):
            for h in range(H):
                cs[:, rp * 16 + h] = LAM_CORE[h] ** (rank - 1 - rp)
        csins.append(cs.astype(np.float32))
    return cb.astype(ml_dtypes.bfloat16), cf.astype(np.float32), csins


class Sem:
    __slots__ = ("h", "n", "name")

    def __init__(self, nc, stack, name):
        self.h = stack.enter_context(nc.semaphore(name))
        self.n = 0
        self.name = name


class Em:
    ENGS = ("sync", "gpsimd", "tensor", "vector", "scalar")

    def __init__(self, nc):
        self.nc = nc
        self.q = {e: [] for e in self.ENGS}

    def raw(self, eng, fn):
        self.q[eng].append(fn)

    def op(self, eng, make, waits=(), incs=()):
        wb = tuple((s.h, int(v)) for (s, v) in waits if int(v) > 0)
        ib = []
        for (s, k) in incs:
            s.n += k
            ib.append((s.h, k))
        assert len(ib) <= 1, "hardware allows one semaphore update per instruction"
        ib = tuple(ib)

        def f(e, make=make, wb=wb, ib=ib):
            for hh, v in wb:
                e.wait_ge(hh, v)
            inst = make(e)
            for hh, k in ib:
                inst.then_inc(hh, k)
        self.q[eng].append(f)

    def dma(self, eng, out, in_, sem, waits=(), **kw):
        self.op(eng, lambda e, out=out, in_=in_, kw=kw: e.dma_start(out=out, in_=in_, **kw),
                waits=waits, incs=((sem, 16),))
        return sem.n

    def wait(self, eng, sem, val):
        self.op(eng, lambda e: e.nop(), waits=((sem, val),))

    def flush(self, block):
        for name in self.ENGS:
            fns = self.q[name]
            getattr(block, name)(lambda e, fns=fns: [f(e) for f in fns])


class Bank:

    def __init__(self, nc, stack, em, i):
        self.ap = stack.enter_context(nc.psum_tensor(f"bank{i}", [128, 512], F32))
        self.s_mm = Sem(nc, stack, f"b{i}_mm")
        self.s_ev = Sem(nc, stack, f"b{i}_ev")
        self.em = em

    @property
    def bf(self):
        return self.ap.bitcast(BF16)

    def mms(self, makes, extra_waits=(), extra_incs=()):
        n = len(makes)
        for i, mk in enumerate(makes):
            w = (((self.s_ev, self.s_ev.n),) + tuple(extra_waits)) if i == 0 else ()
            inc = (((self.s_mm, 1),) + tuple(extra_incs)) if i == n - 1 else ()
            self.em.op("tensor", mk, waits=w, incs=inc)

    def ev(self, eng, make, extra_waits=(), extra_incs=()):
        self.em.op(eng, make,
                   waits=((self.s_mm, self.s_mm.n),) + tuple(extra_waits),
                   incs=((self.s_ev, 1),) + tuple(extra_incs))


def build(replica_groups, debug=False):
    nc = bass.Bass("TRN2", target_bir_lowering=False)
    group = len(replica_groups[0])

    x = nc.dram_tensor("x", [TOK, D], F32, kind="ExternalInput").ap()
    w_qkv = nc.dram_tensor("w_qkv", [QKVO, D], F32, kind="ExternalInput").ap()
    b_qkv = nc.dram_tensor("b_qkv", [QKVO], F32, kind="ExternalInput").ap()
    w_out = nc.dram_tensor("w_out", [D, D], F32, kind="ExternalInput").ap()
    b_out = nc.dram_tensor("b_out", [D], F32, kind="ExternalInput").ap()
    W1 = nc.dram_tensor("W1", [FH, D], F32, kind="ExternalInput").ap()
    W2 = nc.dram_tensor("W2", [D, FH], F32, kind="ExternalInput").ap()
    W3 = nc.dram_tensor("W3", [FH, D], F32, kind="ExternalInput").ap()
    cb_d = nc.dram_tensor("cbE", [128, CB_COLS], BF16, kind="ExternalInput").ap()
    cf_d = nc.dram_tensor("cf", [128, CF_COLS], F32, kind="ExternalInput").ap()
    csin_d = nc.dram_tensor("csin", [128, 64], F32, kind="ExternalInput").ap()
    out = nc.dram_tensor("out", [TOK, D], F32, kind="ExternalOutput").ap()

    x_fmd = nc.dram_tensor("x_fmd", [D, TOK], BF16).ap()
    q_bfd = nc.dram_tensor("q_bfd", [D, TOK], BF16).ap()
    k_bfd = nc.dram_tensor("k_bfd", [D, TOK], BF16).ap()
    y_bfd = nc.dram_tensor("y_bfd", [D, TOK], BF16).ap()
    wout_t = nc.dram_tensor("wout_t", [D, D], BF16).ap()
    w1t = nc.dram_tensor("w1t", [FH, D], BF16).ap()
    w3t = nc.dram_tensor("w3t", [FH, D], BF16).ap()
    w2sb = nc.dram_tensor("w2sb", [D, FH], BF16).ap()
    w2p = nc.dram_tensor("w2p", [D, FH], BF16).ap()
    A_dram = nc.dram_tensor("A_dram", [H * 128, 128], F32).ap()
    A_gath = nc.dram_tensor("A_gath", [group * H * 128, 128], F32).ap()

    dbg = {}
    if debug:
        for nm, shp, dt in (("dbg_qfm", [D, TOK], BF16), ("dbg_kfm", [D, TOK], BF16),
                            ("dbg_A", [H * 128, 128], F32), ("dbg_y", [D, TOK], BF16),
                            ("dbg_w2p", [D, FH], BF16), ("dbg_att", [128, 8192], BF16),
                            ("dbg_res", [128, 8192], F32),
                            ("dbg_rows", [4, 2048], F32), ("dbg_rows3", [8, 2048], F32)):
            dbg[nm] = nc.dram_tensor(nm, shp, dt, kind="ExternalOutput").ap()

    em = Em(nc)

    with contextlib.ExitStack() as st:
        E = st.enter_context

        cf = E(nc.sbuf_tensor("cf_sb", [128, CF_COLS], F32))
        csin = E(nc.sbuf_tensor("csin_sb", [128, 64], F32))
        ones_bf = E(nc.sbuf_tensor("ones_bf", [128, 128], BF16))
        bias_stage = E(nc.sbuf_tensor("bias_stage", [64, 128], F32))
        bias_sb = E(nc.sbuf_tensor("bias_sb", [128, 64], F32))
        bqkv = bias_sb[:, 0:48]
        bout = bias_sb[:, 48:64]

        B = [Bank(nc, st, em, i) for i in range(8)]

        s_ld = Sem(nc, st, "s_ld")
        s_wl = [Sem(nc, st, "s_wl0"), Sem(nc, st, "s_wl1")]
        s_ps = [Sem(nc, st, "s_ps0"), Sem(nc, st, "s_ps1")]
        s_st = Sem(nc, st, "s_st")
        s_wpp = [Sem(nc, st, "s_wpp0"), Sem(nc, st, "s_wpp1")]
        s_out = Sem(nc, st, "s_out")
        s_cc = Sem(nc, st, "s_cc")
        s_ev = Sem(nc, st, "s_ev")
        s_misc = Sem(nc, st, "s_misc")
        s_xfm = Sem(nc, st, "s_xfm")
        s_xs = Sem(nc, st, "s_xs")
        s_dbg = Sem(nc, st, "s_dbg")
        s_pld = [Sem(nc, st, f"s_pld{i}") for i in range(4)]
        s_pst4 = [Sem(nc, st, f"s_pst{i}") for i in range(4)]
        s_gl = Sem(nc, st, "s_gl")
        s_q = Sem(nc, st, "s_q")
        s_kq = [Sem(nc, st, "s_kq0"), Sem(nc, st, "s_kq1")]
        s_kx = Sem(nc, st, "s_kx")
        s_wop = [Sem(nc, st, "s_wop0"), Sem(nc, st, "s_wop1")]
        s_y = Sem(nc, st, "s_y")
        s_sbf = Sem(nc, st, "s_sbf")
        s_w1p = [Sem(nc, st, "s_w1p0"), Sem(nc, st, "s_w1p1")]
        s_w3 = Sem(nc, st, "s_w3")
        s_w2p = [Sem(nc, st, "s_w2p0"), Sem(nc, st, "s_w2p1")]
        s_yg = Sem(nc, st, "s_yg")
        s_b_ld = [Sem(nc, st, "s_b_ld0"), Sem(nc, st, "s_b_ld1")]
        s_b_st = [Sem(nc, st, "s_b_st0"), Sem(nc, st, "s_b_st1")]
        s_a_ld = [Sem(nc, st, "s_a_ld0"), Sem(nc, st, "s_a_ld1")]
        s_a_st = [Sem(nc, st, "s_a_st0"), Sem(nc, st, "s_a_st1")]

        em.dma("sync", cf[:, :], cf_d, s_ld)
        em.dma("sync", csin[:, :], csin_d, s_ld)
        em.dma("sync", bias_stage[0:48, :], b_qkv.rearrange("(c p) -> c p", p=128), s_ld)
        em.dma("sync", bias_stage[48:64, :], b_out.rearrange("(c p) -> c p", p=128), s_ld)

        with contextlib.ExitStack() as stE:
            EE = stE.enter_context
            cb1 = EE(nc.sbuf_tensor("cb1", [128, 192], BF16))
            v_tm = EE(nc.sbuf_tensor("v_tm", [128, NCH * H * 128], BF16))

            em.dma("sync", cb1[:, 0:128], cb_d[:, CB_ID:CB_ID + 128], s_ld)
            em.dma("sync", cb1[:, 128:192], cb_d[:, CB_H64:CB_H64 + 64], s_ld)
            ld_all = s_ld.n
            em.op("vector", lambda e: e.memset(ones_bf[:, :], 1.0),
                  waits=((s_ld, ld_all),), incs=((s_ev, 1),))
            B[7].mms([lambda e, o_=B[7].ap[:, 0:64], i_=bias_stage[0:64, :],
                      id_=cf[0:64, CF_ID:CF_ID + 64]: e.transpose(o_, i_, id_)],
                     extra_waits=((s_ld, ld_all),))
            B[7].ev("vector", lambda e, d=bias_sb[:, :], s=B[7].ap:
                    e.tensor_copy(d, s[:, 0:64]))
            bias_ready = (B[7].s_ev, B[7].s_ev.n)
            ID_BF = cb1[:, 0:128]
            cb1H64 = cb1[0:64, 128:192]
            ID_F = cf[:, CF_ID:CF_ID + 128]
            ONES_COL_F = cf[:, CF_ONES:CF_ONES + 1]
            ONES_ROW_F = cf[0:1, CF_ONES:CF_ONES + 128]

            trpool = [B[4], B[5], B[6]]
            _tr = [0]

            def next_tr():
                _tr[0] = (_tr[0] + 1) % 3
                return trpool[_tr[0]]

            wst_free = {0: (), 1: ()}
            _wslot = [0]

            def wtr_slice(srcW, r0, dst4_fn, first_evac_waits=(), src_cols=D):
                par = _wslot[0] % 2
                _wslot[0] += 1
                stg = wstage[:, par * D:par * D + src_cols]
                em.dma("sync", stg, srcW[r0:r0 + 128, :], s_wl[par],
                       waits=wst_free[par])
                ld = (s_wl[par], s_wl[par].n)
                for q in range(src_cols // 512):
                    tb = next_tr()
                    makes = []
                    for i4 in range(4):
                        kc = q * 4 + i4
                        makes.append(
                            lambda e,
                            o_=tb.ap.bitcast(F32)[:, i4 * 128:(i4 + 1) * 128],
                            i_=stg[:, kc * 128:(kc + 1) * 128], id_=ID_F:
                            e.transpose(o_, i_, id_))
                    tb.mms(makes, extra_waits=(ld,) if q == 0 else ())
                    exw = first_evac_waits if q == 0 else ()
                    tb.ev("vector", lambda e, d=dst4_fn(q),
                          s=tb.ap.bitcast(F32)[:, 0:512]
                          .rearrange("p (a b) -> p a b", a=4):
                          e.tensor_copy(d, s),
                          extra_waits=exw)
                wst_free[par] = tuple((bk.s_mm, bk.s_mm.n) for bk in trpool)
                return tuple((bk.s_ev, bk.s_ev.n) for bk in trpool)

            prep_slices = []
            _pslot = [0]

            _pidx = [0]

            def mk_prep(srcW, dst_t, ROWS):
                for oc in range(ROWS // 128):
                    ppar = _pidx[0] % 2
                    _pidx[0] += 1

                    def slice_fn(oc=oc, srcW=srcW, dst_t=dst_t, ppar=ppar):
                        pst = prepstage[:, ppar * D:(ppar + 1) * D]
                        caps = wtr_slice(
                            srcW, oc * 128,
                            lambda q, pst=pst: pst[:, q * 512:(q + 1) * 512]
                            .rearrange("p (a b) -> p a b", a=4),
                            first_evac_waits=((s_ps[ppar], s_ps[ppar].n),))
                        em.dma("gpsimd", dst_t[oc * 128:(oc + 1) * 128, :],
                               pst[:, :], s_ps[ppar], waits=caps)
                    prep_slices.append(slice_fn)

            mk_prep(w_out, wout_t, D)
            prep_wout_done = ((s_ps[0], (_pidx[0] + 1) // 2 * 16),
                              (s_ps[1], _pidx[0] // 2 * 16))
            mk_prep(W1, w1t, FH)
            mk_prep(W3, w3t, FH)
            prep_w13_done = ((s_ps[0], (_pidx[0] + 1) // 2 * 16),
                             (s_ps[1], _pidx[0] // 2 * 16))

            w2view = W2.rearrange("d (h1 h0) -> h1 d h0", h1=64)
            w2sbv = w2sb.rearrange("d (h1 h0) -> h1 d h0", h1=64)

            def b_slice(ds, ppar):
                par = _wslot[0] % 2
                _wslot[0] += 1
                pst = prepstage[:, ppar * D:(ppar + 1) * D]
                stg = wstage[0:64, par * D:(par + 1) * D]
                em.dma("sync", stg.rearrange("p (a b) -> p a b", a=16),
                       w2view[:, ds * 16:(ds + 1) * 16, :], s_wl[par],
                       waits=wst_free[par])
                ld = (s_wl[par], s_wl[par].n)
                em.op("scalar", lambda e, d=pst[0:64, :], s=stg:
                      e.activation(d, s, AF.Copy),
                      waits=(ld, (s_ps[ppar], s_ps[ppar].n)), incs=((s_misc, 1),))
                cast_r = s_misc.n
                caps = []
                for q4 in range(4):
                    tb = next_tr()
                    tb.mms([lambda e, o_=tb.ap[0:64, :], l_=cb1H64,
                            r_=pst[0:64, q4 * 512:(q4 + 1) * 512]:
                            e.matmul(o_, l_, r_, start=True, stop=True)],
                           extra_waits=((s_misc, cast_r),) if q4 == 0 else ())
                    tb.ev("vector",
                          lambda e, d=pst[64:128, q4 * 512:(q4 + 1) * 512],
                          s=tb.ap: e.tensor_copy(d, s[0:64, 0:512]))
                    caps.append((tb.s_ev, tb.s_ev.n))
                em.dma("gpsimd", w2sbv[:, ds * 16:(ds + 1) * 16, :],
                       pst[64:128, :].rearrange("p (a b) -> p a b", a=16),
                       s_ps[ppar], waits=tuple(caps))
                wst_free[par] = tuple((bk.s_mm, bk.s_mm.n) for bk in trpool)

            for ds in range(128):
                ppar = _pidx[0] % 2
                _pidx[0] += 1
                prep_slices.append(lambda ds=ds, ppar=ppar: b_slice(ds, ppar))
            prep_b_done = ((s_ps[0], (_pidx[0] + 1) // 2 * 16),
                           (s_ps[1], _pidx[0] // 2 * 16))

            def pump(n):
                for _ in range(n):
                    if prep_slices:
                        prep_slices.pop(0)()

            A_sb = EE(nc.sbuf_tensor("A_sb", [128, H * 128], F32))
            stP1 = contextlib.ExitStack()
            E1 = stP1.enter_context
            wstage = E1(nc.sbuf_tensor("wstage", [128, 2 * D], F32))
            prepstage = E1(nc.sbuf_tensor("prepstage", [128, 2 * D], BF16))
            x_fm = E1(nc.sbuf_tensor("x_fm", [128, NKC * TOK], BF16))
            for tt in range(NCH):
                fw = ((s_ld, ld_all),) if tt == 0 else ()
                xfm_done = wtr_slice(
                    x, tt * 128,
                    lambda q, tt=tt:
                    x_fm.rearrange("p (kc t) -> p kc t", t=TOK)
                    [:, q * 4:(q + 1) * 4, tt * 128:tt * 128 + 128],
                    first_evac_waits=fw)
            for kc in range(NKC):
                em.dma("gpsimd", x_fmd[kc * 128:(kc + 1) * 128, :],
                       x_fm[:, kc * TOK:(kc + 1) * TOK], s_xs,
                       waits=xfm_done if kc == 0 else ())
            xfmd_done = s_xs.n

            wpan = [E1(nc.sbuf_tensor(f"wpan{i}", [128, NKC * 512], BF16)) for i in range(2)]
            pstage = [E1(nc.sbuf_tensor(f"pstage{i}", [128, TOK], BF16)) for i in range(2)]
            kdecT4 = E1(nc.sbuf_tensor("kdecT4", [128, 1024], BF16))
            v_tm4 = v_tm.rearrange("p (j x) -> p j x", x=H * 128)

            projB = [B[0], B[1], B[2], B[3]]
            AB = B[7]

            o_order = list(range(32, 48)) + list(range(16, 32)) + list(range(16))

            npan = len(o_order) // 4
            pan_ld = {}

            def load_panel(p, extra=()):
                buf = wpan[p % 2]
                caps = ()
                for oc4 in range(4):
                    o0 = o_order[4 * p + oc4] * 128
                    fw = tuple(extra) if oc4 == 0 else ()
                    caps = wtr_slice(
                        w_qkv, o0,
                        lambda q, oc4=oc4, buf=buf:
                        buf.rearrange("p (kc o) -> p kc o", o=512)
                        [:, q * 4:(q + 1) * 4, oc4 * 128:oc4 * 128 + 128],
                        first_evac_waits=fw)
                pan_ld[p] = caps

            load_panel(0)
            load_panel(1)

            pstage_free = {0: (), 1: ()}
            vtm_caps = ()

            for i_o, o in enumerate(o_order):
                p, par = i_o // 4, i_o % 2
                buf = wpan[p % 2]
                banks = [projB[t] for t in range(4)]
                for t in range(4):
                    makes = []
                    for kc in range(NKC):
                        lhsT = buf[:, kc * 512 + (i_o % 4) * 128:kc * 512 + (i_o % 4) * 128 + 128]
                        rhs = x_fm[:, kc * TOK + t * 512:kc * TOK + t * 512 + 512]
                        makes.append(lambda e, o_=banks[t].ap, l_=lhsT, r_=rhs,
                                     st_=(kc == 0), sp_=(kc == NKC - 1):
                                     e.matmul(o_[:, :], l_, r_, start=st_, stop=sp_))
                    exw = list(pan_ld[p])
                    if i_o == 0 and t == 0:
                        exw.extend(xfm_done)
                        exw.append((s_ld, ld_all))
                    banks[t].mms(makes, extra_waits=tuple(exw))
                if i_o % 4 == 3 and p + 2 < npan:
                    load_panel(p + 2, extra=((banks[3].s_mm, banks[3].s_mm.n),))
                for t in range(4):
                    exw = pstage_free[par] if t == 0 else ()
                    if i_o == 0 and t == 0:
                        exw = exw + ((s_ld, ld_all), bias_ready)
                    banks[t].ev("scalar",
                                lambda e, b=banks[t].ap,
                                d=pstage[par][:, t * 512:(t + 1) * 512],
                                bias=bqkv[:, o:o + 1]:
                                e.activation(d, b[:, :], AF.Identity, bias=bias, scale=1.0),
                                extra_waits=exw)
                my_pst = tuple((bk.s_ev, bk.s_ev.n) for bk in banks)

                typ = "v" if o >= 32 else ("k" if o >= 16 else "q")
                h = o - 32 if typ == "v" else (o - 16 if typ == "k" else o)

                if typ == "v":
                    for q4 in range(4):
                        tb = next_tr()
                        makes = []
                        for i4 in range(4):
                            j = q4 * 4 + i4
                            makes.append(lambda e, o_=tb.bf[:, i4 * 128:(i4 + 1) * 128],
                                         i_=pstage[par][:, j * 128:(j + 1) * 128],
                                         id_=ID_BF: e.transpose(o_, i_, id_))
                        tb.mms(makes, extra_waits=my_pst if q4 == 0 else ())
                        tb.ev("vector",
                              lambda e,
                              d=v_tm4[:, q4 * 4:(q4 + 1) * 4, h * 128:(h + 1) * 128],
                              s=tb.bf[:, 0:512].rearrange("p (a b) -> p a b", a=4):
                              e.tensor_copy(d, s))
                    pstage_free[par] = tuple((bk.s_mm, bk.s_mm.n) for bk in trpool)
                    if i_o == 15:
                        vtm_caps = tuple((bk.s_ev, bk.s_ev.n) for bk in trpool)
                elif typ == "k":
                    for q4 in range(4):
                        tb = next_tr()
                        makes = []
                        for i4 in range(4):
                            j = q4 * 4 + i4
                            makes.append(lambda e, o_=tb.bf[:, i4 * 128:(i4 + 1) * 128],
                                         i_=pstage[par][:, j * 128:(j + 1) * 128],
                                         id_=ID_BF: e.transpose(o_, i_, id_))
                        tb.mms(makes, extra_waits=my_pst if q4 == 0 else ())
                        half = (q4 % 2) * 512
                        tb.ev("vector",
                              lambda e,
                              d=kdecT4[:, half:half + 512]
                              .rearrange("p (a b) -> p a b", a=4),
                              s=tb.bf[:, 0:512].rearrange("p (a b) -> p a b", a=4),
                              m=cf[:, CF_DKG + h * 16 + q4 * 4:
                                   CF_DKG + h * 16 + q4 * 4 + 4]
                              .unsqueeze(2).broadcast_to([128, 4, 128]):
                              e.tensor_tensor(d, s, m, ALU.mult),
                              extra_waits=((AB.s_mm, AB.s_mm.n),))
                        exw = [(tb.s_ev, tb.s_ev.n)]
                        if q4 == 0:
                            exw.extend(vtm_caps)
                        for i4 in range(4):
                            j = q4 * 4 + i4
                            AB.mms([lambda e, o_=AB.ap[:, 0:128],
                                    l_=kdecT4[:, half + i4 * 128:half + (i4 + 1) * 128],
                                    r_=v_tm[:, (j * H + h) * 128:(j * H + h) * 128 + 128],
                                    st_=(j == 0), sp_=(j == NCH - 1):
                                    e.matmul(o_, l_, r_, start=st_, stop=sp_)],
                                   extra_waits=tuple(exw) if i4 == 0 else ())
                    AB.ev("vector", lambda e, d=A_sb[:, h * 128:(h + 1) * 128],
                          s=AB.ap: e.tensor_copy(d, s[:, 0:128]))
                    em.dma("gpsimd", k_bfd[h * 128:(h + 1) * 128, :], pstage[par][:, :],
                           s_kq[par], waits=my_pst)
                    pstage_free[par] = tuple((bk.s_mm, bk.s_mm.n) for bk in trpool) + \
                        ((s_kq[par], s_kq[par].n),)
                else:
                    em.dma("gpsimd", q_bfd[h * 128:(h + 1) * 128, :], pstage[par][:, :],
                           s_kq[par], waits=my_pst)
                    pstage_free[par] = ((s_kq[par], s_kq[par].n),)
                if o == 31:
                    A_done = (AB.s_ev, AB.s_ev.n)
                    em.dma("gpsimd", A_dram.rearrange("(h p) c -> p h c", p=128),
                           A_sb[:, :].rearrange("p (h c) -> p h c", h=H),
                           s_st, waits=(A_done,))
                    em.op("gpsimd", lambda e: e.collective_compute(
                        "AllGather", ALU.bypass, replica_groups=replica_groups,
                        ins=[A_dram.opt()], outs=[A_gath.opt()]),
                        waits=((s_st, s_st.n),), incs=((s_cc, 1),))
                pump(8)

            pump(len(prep_slices))
            kq_stores = ((s_kq[0], s_kq[0].n), (s_kq[1], s_kq[1].n))

            if debug:
                em.dma("gpsimd", dbg["dbg_A"], A_dram, s_dbg,
                       waits=((s_dbg, s_dbg.n),))
            stP1.close()
            for bk in B:
                em.wait("sync", bk.s_mm, bk.s_mm.n)
                em.wait("sync", bk.s_ev, bk.s_ev.n)
            em.wait("sync", s_xs, xfmd_done)
            em.wait("sync", s_kq[0], s_kq[0].n)
            em.wait("sync", s_kq[1], s_kq[1].n)
            em.wait("sync", s_st, s_st.n)
            em.wait("sync", s_wl[0], s_wl[0].n)
            em.wait("sync", s_wl[1], s_wl[1].n)
            em.wait("sync", s_ps[0], s_ps[0].n)
            em.wait("sync", s_ps[1], s_ps[1].n)

            with contextlib.ExitStack() as st2:
                E2 = st2.enter_context
                cb2 = E2(nc.sbuf_tensor("cb2", [128, CB_COLS - CB_MASK], BF16))
                em.dma("sync", cb2[:, :], cb_d[:, CB_MASK:CB_COLS], s_ld)
                cb2_ld = s_ld.n
                S = E2(nc.sbuf_tensor("S", [128, H * 128], F32))
                S_bf_all = E2(nc.sbuf_tensor("S_bf_all", [128, H * 128], BF16))
                q_pan = E2(nc.sbuf_tensor("q_pan", [128, NKC * 512], BF16))
                k_pan = E2(nc.sbuf_tensor("k_pan", [128, NKC * 512], BF16))
                attn_fm = E2(nc.sbuf_tensor("attn_fm", [128, NKC * 512], BF16))
                resid1 = E2(nc.sbuf_tensor("resid1", [128, NKC * 512], F32))
                woutb = [E2(nc.sbuf_tensor(f"woutb{i}", [128, NKC * 128], BF16))
                         for i in range(2)]
                sq_scr = E2(nc.sbuf_tensor("sq_scr", [128, 512], F32))
                rows = E2(nc.sbuf_tensor("rows", [1, 4 * 512], F32))
                qdec_all = E2(nc.sbuf_tensor("qdec_all", [128, H * 128], BF16))
                smask_all = E2(nc.sbuf_tensor("smask_all", [128, H * 128], BF16))
                kd2_all = E2(nc.sbuf_tensor("kd2_all", [128, H * 128], BF16))
                q_pv = q_pan.rearrange("p (h t) -> p h t", t=512)
                k_pv = k_pan.rearrange("p (h t) -> p h t", t=512)
                at_v = attn_fm.rearrange("p (h t) -> p h t", t=512)
                S_v = S.rearrange("p (h c) -> p h c", c=128)
                dq_v = cb2.rearrange("p (b c) -> p b c", c=128)
                cb2_m4 = cb2[:, 0:H * 128]

                for g in range(group):
                    em.dma("sync",
                           resid1[:, g * 2048:(g + 1) * 2048]
                           .rearrange("p (h c) -> p h c", h=H),
                           A_gath[g * 2048:(g + 1) * 2048, :]
                           .rearrange("(h p) c -> p h c", p=128),
                           s_gl, waits=((s_cc, s_cc.n),) if g == 0 else ())
                gl_done = s_gl.n
                for h in range(H):
                    em.op("vector", lambda e, d=S[:, h * 128:(h + 1) * 128],
                          a=resid1[:, h * 128:(h + 1) * 128], sc=csin[:, h:h + 1]:
                          e.tensor_scalar_mul(d, a, sc),
                          waits=((s_gl, gl_done),) if h == 0 else
                                ((s_misc, s_misc.n),),
                          incs=((s_misc, 1),))
                    for g in range(1, group):
                        em.op("vector", lambda e, d=S[:, h * 128:(h + 1) * 128],
                              a=resid1[:, (g * 2048 + h * 128):(g * 2048 + h * 128) + 128],
                              sc=csin[:, g * 16 + h:g * 16 + h + 1]:
                              e.scalar_tensor_tensor(d, a, sc, d, ALU.mult, ALU.add),
                              waits=((s_misc, s_misc.n),), incs=((s_misc, 1),))
                S_ready = (s_misc, s_misc.n)

                SA = [B[0], B[1], B[2], B[3]]
                KX = [B[4], B[5], B[6], B[7]]
                opB = [B[0], B[1]]
                stB = [B[2], B[3]]
                bcB = [B[4], B[5]]

                supd_cap = ()
                rows_dbg_gate = ()
                attn_free = ()
                ktr_caps = ()
                kpan_free = ()
                resid_free = ()

                qdec_last = 0
                for g2 in range(4):
                    t0 = g2 * 512
                    w = list(kq_stores) if g2 == 0 else \
                        list(mixer_done) + [(s_misc, qdec_last)]
                    em.dma("sync", q_pan[:, :],
                           q_bfd.rearrange("(c p) t -> p c t", p=128)[:, :, t0:t0 + 512],
                           s_q, waits=tuple(w))
                    em.dma("sync", k_pan[:, :],
                           k_bfd.rearrange("(c p) t -> p c t", p=128)[:, :, t0:t0 + 512],
                           s_kx, waits=kpan_free)
                    qk_done = ((s_q, s_q.n), (s_kx, s_kx.n))

                    for jj in range(4):
                        j = g2 * 4 + jj
                        first = (jj == 0)
                        w_s = list(supd_cap)
                        w_s += [(bk.s_mm, bk.s_mm.n) for bk in SA]
                        if g2 == 0 and first:
                            w_s += [S_ready, (s_ld, cb2_ld)]
                        em.op("scalar", lambda e, d=S_bf_all[:, :], s=S[:, :]:
                              e.copy(d, s), waits=tuple(w_s), incs=((s_sbf, 1),))
                        sbf_ready = s_sbf.n
                        em.op("vector", lambda e,
                              d=qdec_all.rearrange("p (h c) -> p h c", c=128),
                              a=q_pv[:, :, jj * 128:jj * 128 + 128],
                              m=dq_v[:, H:2 * H, :]:
                              e.tensor_tensor(d, a, m, ALU.mult),
                              waits=(qk_done if first else ()) +
                                    tuple((bk.s_mm, bk.s_mm.n) for bk in SA) +
                                    (((s_ld, cb2_ld),)
                                     if (g2 == 0 and first) else ()),
                              incs=((s_misc, 1),))
                        qdec_ready = qdec_last = s_misc.n
                        for b4 in range(4):
                            kb = KX[b4]
                            makes = []
                            for i4 in range(4):
                                hh = 4 * b4 + i4
                                makes.append(lambda e,
                                             o_=kb.bf[:, i4 * 128:(i4 + 1) * 128],
                                             i_=k_pv[:, hh, jj * 128:jj * 128 + 128],
                                             id_=ID_BF: e.transpose(o_, i_, id_))
                            kb.mms(makes, extra_waits=qk_done
                                   if (first and b4 == 0) else ())
                            kb.ev("vector", lambda e,
                                  d=kd2_all[:, b4 * 512:(b4 + 1) * 512]
                                  .rearrange("p (a b) -> p a b", a=4),
                                  s=kb.bf[:, 0:512].rearrange("p (a b) -> p a b", a=4),
                                  m=cf[:, CF_DKL + 4 * b4:CF_DKL + 4 * b4 + 4]
                                  .unsqueeze(2).broadcast_to([128, 4, 128]):
                                  e.tensor_tensor(d, s, m, ALU.mult))
                        for b4 in range(4):
                            kb = KX[b4]
                            makes = []
                            for i4 in range(4):
                                hh = 4 * b4 + i4
                                makes.append(lambda e,
                                             o_=kb.ap[:, i4 * 128:(i4 + 1) * 128],
                                             l_=kd2_all[:, hh * 128:(hh + 1) * 128],
                                             r_=v_tm[:, (j * H + hh) * 128:
                                                     (j * H + hh) * 128 + 128]:
                                             e.matmul(o_, l_, r_, start=True, stop=True))
                            kb.mms(makes)
                            em.op("vector", lambda e,
                                  d=S_v[:, 4 * b4:4 * b4 + 4, :],
                                  m=cf[:, CF_LAMC + 4 * b4:CF_LAMC + 4 * b4 + 4]
                                  .unsqueeze(2).broadcast_to([128, 4, 128]):
                                  e.tensor_tensor(d, d, m, ALU.mult),
                                  waits=((s_sbf, sbf_ready),) if b4 == 0 else (),
                                  incs=((s_misc, 1),))
                            kb.ev("vector", lambda e,
                                  d=S_v[:, 4 * b4:4 * b4 + 4, :],
                                  s=kb.ap[:, 0:512].rearrange("p (a b) -> p a b", a=4):
                                  e.tensor_tensor(d, d, s, ALU.add),
                                  extra_waits=((s_misc, s_misc.n),))
                        supd_cap = tuple((bk.s_ev, bk.s_ev.n) for bk in KX)
                        for b4 in range(4):
                            sb = SA[b4]
                            makes = []
                            for i4 in range(4):
                                hh = 4 * b4 + i4
                                makes.append(lambda e,
                                             o_=sb.ap[:, i4 * 128:(i4 + 1) * 128],
                                             l_=k_pv[:, hh, jj * 128:jj * 128 + 128],
                                             r_=q_pv[:, hh, jj * 128:jj * 128 + 128]:
                                             e.matmul(o_, l_, r_, start=True, stop=True))
                            sb.mms(makes, extra_waits=qk_done
                                   if (first and b4 == 0) else ())
                            sb.ev("vector", lambda e,
                                  d=smask_all[:, b4 * 512:(b4 + 1) * 512],
                                  s=sb.ap[:, 0:512],
                                  m=cb2_m4[:, b4 * 512:(b4 + 1) * 512]:
                                  e.tensor_tensor(d, s, m, ALU.mult))
                        for b4 in range(4):
                            ab = SA[b4]
                            makes = []
                            for i4 in range(4):
                                hh = 4 * b4 + i4
                                makes.append(lambda e,
                                             o_=ab.ap[:, i4 * 128:(i4 + 1) * 128],
                                             l_=v_tm[:, (j * H + hh) * 128:
                                                     (j * H + hh) * 128 + 128],
                                             r_=smask_all[:, hh * 128:(hh + 1) * 128]:
                                             e.matmul(o_, l_, r_, start=True, stop=False))
                                makes.append(lambda e,
                                             o_=ab.ap[:, i4 * 128:(i4 + 1) * 128],
                                             l_=S_bf_all[:, hh * 128:(hh + 1) * 128],
                                             r_=qdec_all[:, hh * 128:(hh + 1) * 128]:
                                             e.matmul(o_, l_, r_, start=False, stop=True))
                            ab.mms(makes, extra_waits=((s_sbf, sbf_ready),
                                                       (s_misc, qdec_ready)))
                            exw = attn_free if (first and b4 == 0) else ()
                            ab.ev("scalar", lambda e,
                                  d=at_v[:, 4 * b4:4 * b4 + 4,
                                         jj * 128:jj * 128 + 128],
                                  s=ab.ap[:, 0:512].rearrange("p (a b) -> p a b", a=4):
                                  e.copy(d, s), extra_waits=exw)
                    mixer_done = tuple((bk.s_ev, bk.s_ev.n) for bk in SA)
                    ktr_caps = tuple((bk.s_mm, bk.s_mm.n) for bk in KX)

                    em.dma("sync", k_pan[:, :],
                           x_fmd.rearrange("(c p) t -> p c t", p=128)[:, :, t0:t0 + 512],
                           s_kx, waits=mixer_done + ((s_xs, xfmd_done),) + ktr_caps)
                    xp_done = (s_kx, s_kx.n)
                    r_readys = {}
                    for dc in range(NKC):
                        wb = woutb[dc % 2]
                        w = list(prep_wout_done) if (g2 == 0 and dc < 2) else []
                        if g2 > 0 or dc >= 2:
                            w.append((opB[dc % 2].s_mm, opB[dc % 2].s_mm.n))
                        em.dma("sync", wb[:, :],
                               wout_t[dc * 128:(dc + 1) * 128, :], s_wop[dc % 2],
                               waits=tuple(w))
                        wo_ready = (s_wop[dc % 2], s_wop[dc % 2].n)
                        ob = opB[dc % 2]
                        makes = []
                        for kc in range(NKC):
                            makes.append(lambda e, o_=ob.ap,
                                         l_=wb[:, kc * 128:(kc + 1) * 128],
                                         r_=attn_fm[:, kc * 512:(kc + 1) * 512],
                                         st_=(kc == 0), sp_=(kc == NKC - 1):
                                         e.matmul(o_[:, :], l_, r_, start=st_, stop=sp_))
                        ob.mms(makes, extra_waits=(wo_ready,) + mixer_done)
                        exw = ()
                        if dc == 0:
                            exw = ((stB[0].s_mm, stB[0].s_mm.n),
                                   (stB[1].s_mm, stB[1].s_mm.n)) + resid_free
                        ob.ev("scalar", lambda e, d=resid1[:, dc * 512:(dc + 1) * 512],
                              s=ob.ap, bias=bout[:, dc:dc + 1]:
                              e.activation(d, s[:, :], AF.Identity, bias=bias, scale=1.0),
                              extra_waits=exw)
                        em.op("vector", lambda e, d=resid1[:, dc * 512:(dc + 1) * 512],
                              a=k_pan[:, dc * 512:(dc + 1) * 512]:
                              e.tensor_tensor(d, d, a, ALU.add),
                              waits=(xp_done, (ob.s_ev, ob.s_ev.n)),
                              incs=((s_misc, 1),))
                        r_ready = s_misc.n
                        r_readys[dc] = r_ready
                        em.op("scalar", lambda e, d=sq_scr[:, :],
                              s=resid1[:, dc * 512:(dc + 1) * 512]:
                              e.activation(d, s, AF.Square),
                              waits=((s_misc, r_ready), (stB[1].s_mm, stB[1].s_mm.n)),
                              incs=((s_misc, 1),))
                        sq_ready = s_misc.n
                        stB[0].mms([lambda e, o_=stB[0].ap[0:1, :], l_=ONES_COL_F,
                                    r_=resid1[:, dc * 512:(dc + 1) * 512],
                                    st_=(dc == 0), sp_=(dc == NKC - 1):
                                    e.matmul(o_, l_, r_, start=st_, stop=sp_)],
                                   extra_waits=((s_misc, r_ready),))
                        stB[1].mms([lambda e, o_=stB[1].ap[0:1, :], l_=ONES_COL_F,
                                    r_=sq_scr[:, :],
                                    st_=(dc == 0), sp_=(dc == NKC - 1):
                                    e.matmul(o_, l_, r_, start=st_, stop=sp_)],
                                   extra_waits=((s_misc, sq_ready),))
                    attn_free = ((opB[0].s_mm, opB[0].s_mm.n), (opB[1].s_mm, opB[1].s_mm.n))
                    kpan_free = ((s_misc, s_misc.n),)
                    dbg_gate = ()
                    if debug and g2 == 0:
                        em.dma("gpsimd", dbg["dbg_att"], attn_fm[:, :], s_dbg,
                               waits=mixer_done + ((s_dbg, s_dbg.n),))
                        em.dma("gpsimd", dbg["dbg_res"], resid1[:, :], s_dbg,
                               waits=((s_misc, r_readys[NKC - 1]),
                                      (s_dbg, s_dbg.n)))
                        dbg_gate = ((s_dbg, s_dbg.n),)

                    Rm, Rv = rows[0:1, 0:512], rows[0:1, 512:1024]
                    Ra, Rb = rows[0:1, 1024:1536], rows[0:1, 1536:2048]
                    stB[0].ev("vector", lambda e, d=Rm, s=stB[0].ap:
                              e.tensor_scalar_mul(d, s[0:1, :], 1.0 / D),
                              extra_waits=((s_misc, s_misc.n),) + rows_dbg_gate)
                    stB[1].ev("vector", lambda e, d=Rv, s=stB[1].ap:
                              e.tensor_scalar_mul(d, s[0:1, :], 1.0 / D))
                    em.op("vector", lambda e, d=Ra, m=Rm: e.tensor_tensor(d, m, m, ALU.mult),
                          waits=((stB[0].s_ev, stB[0].s_ev.n),), incs=((s_misc, 1),))
                    em.op("vector", lambda e, d=Ra: e.tensor_scalar_add(d, d, -EPS),
                          waits=((s_misc, s_misc.n),), incs=((s_misc, 1),))
                    em.op("vector", lambda e, d=Rv, a=Ra:
                          e.tensor_tensor(d, d, a, ALU.subtract),
                          waits=((s_misc, s_misc.n), (stB[1].s_ev, stB[1].s_ev.n)),
                          incs=((s_misc, 1),))
                    var_ready = s_misc.n
                    em.op("scalar", lambda e, d=Ra, v=Rv:
                          e.activation(d, v, AF.Sqrt),
                          waits=((s_misc, var_ready),), incs=((s_misc, 1),))
                    sqrt_ready = s_misc.n
                    em.op("vector", lambda e, d=Ra: e.reciprocal(d, d),
                          waits=((s_misc, sqrt_ready),), incs=((s_misc, 1),))
                    em.op("vector", lambda e, d=Rb, m=Rm, a=Ra:
                          e.scalar_tensor_tensor(d, m, -1.0, a, ALU.mult, ALU.mult),
                          waits=((s_misc, s_misc.n),), incs=((s_misc, 1),))
                    ab_ready = s_misc.n
                    if debug:
                        em.dma("gpsimd", dbg["dbg_rows"][g2:g2 + 1, :], rows[0:1, :],
                               s_dbg, waits=((s_misc, ab_ready), (s_dbg, s_dbg.n)))
                        rows_dbg_gate = ((s_dbg, s_dbg.n),)
                    bcB[0].mms([lambda e, o_=bcB[0].ap, l_=ONES_ROW_F, r_=Ra:
                                e.matmul(o_[:, :], l_, r_, start=True, stop=True)],
                               extra_waits=((s_misc, ab_ready),))
                    bcB[1].mms([lambda e, o_=bcB[1].ap, l_=ONES_ROW_F, r_=Rb:
                                e.matmul(o_[:, :], l_, r_, start=True, stop=True)])
                    resid1_v = resid1.rearrange("p (dc t) -> p dc t", t=512)
                    attn_v3 = attn_fm.rearrange("p (dc t) -> p dc t", t=512)
                    em.op("vector", lambda e, d=resid1_v,
                          a=bcB[0].ap[:, :].unsqueeze(1).broadcast_to([128, NKC, 512]):
                          e.tensor_tensor(d, d, a, ALU.mult),
                          waits=((s_misc, r_readys[NKC - 1]),
                                 (bcB[0].s_mm, bcB[0].s_mm.n),
                                 (bcB[1].s_mm, bcB[1].s_mm.n)) + dbg_gate,
                          incs=((bcB[0].s_ev, 1),))
                    em.op("vector", lambda e, d=attn_v3, r=resid1_v,
                          b=bcB[1].ap[:, :].unsqueeze(1).broadcast_to([128, NKC, 512]):
                          e.tensor_tensor(d, r, b, ALU.add),
                          waits=((bcB[0].s_ev, bcB[0].s_ev.n),),
                          incs=((bcB[1].s_ev, 1),))
                    y_ready = (bcB[1].s_ev, bcB[1].s_ev.n)
                    resid_free = (y_ready,)
                    em.dma("gpsimd",
                           y_bfd.rearrange("(c p) t -> p c t", p=128)[:, :, t0:t0 + 512],
                           attn_fm[:, :], s_y, waits=(y_ready,))
                    attn_free = attn_free + ((s_y, s_y.n),)

                y_stores = s_y.n
                if debug:
                    em.dma("gpsimd", dbg["dbg_qfm"], q_bfd, s_dbg,
                           waits=((s_dbg, s_dbg.n),))
                    em.dma("gpsimd", dbg["dbg_kfm"], k_bfd, s_dbg,
                           waits=((s_dbg, s_dbg.n),))
                    em.dma("gpsimd", dbg["dbg_y"], y_bfd, s_dbg,
                           waits=((s_y, y_stores), (s_dbg, s_dbg.n)))

        with contextlib.ExitStack() as st3:
            E3 = st3.enter_context
            GROUPS = [(0, 768), (768, 768), (1536, 512)]
            GMAX = 768
            hidden = E3(nc.sbuf_tensor("hidden", [128, FHC * GMAX], BF16))
            y_grp = E3(nc.sbuf_tensor("y_grp", [128, NKC * GMAX], BF16))
            u1 = E3(nc.sbuf_tensor("u1", [128, 3 * NKC * 512], BF16))
            w1pan = u1[:, 0:NKC * 512]
            w1pre = u1[:, NKC * 512:2 * NKC * 512]
            w3pan = u1[:, 2 * NKC * 512:3 * NKC * 512]
            u_res2 = u1[:, 0:NKC * GMAX]
            sq3 = u1[:, 12288:12800]
            u1f = u1.bitcast(F32)
            apply2 = [u1f[:, 6400:6912], u1f[:, 6912:7424]]
            stag = u1f[:, 7424:9472]
            rows3 = u1f[0:1, 9472:11520]
            w2pan = [E3(nc.sbuf_tensor(f"w2pan{i}", [128, FHC * 128], BF16))
                     for i in range(2)]
            aSlot = [w2pan[0][:, 6144:8192], w2pan[1][:, 6144:8192]]
            cbE2 = E3(nc.sbuf_tensor("cbE2", [128, 1216], BF16))
            silu_scr = [cbE2[:, 192:704], cbE2[:, 704:1216]]

            for bk in B:
                em.wait("sync", bk.s_mm, bk.s_mm.n)
                em.wait("sync", bk.s_ev, bk.s_ev.n)
            em.wait("sync", s_misc, s_misc.n)
            em.wait("sync", s_sbf, s_sbf.n)
            em.wait("sync", s_y, s_y.n)
            em.wait("sync", s_st, s_st.n)
            em.wait("sync", s_dbg, s_dbg.n)
            for s4 in s_pst4:
                em.wait("sync", s4, s4.n)
            em.dma("sync", cbE2[:, 64:192], cb_d[:, CB_H128:CB_H128 + 128], s_ld)
            cbE2_ld = s_ld.n

            hbB = [B[0], B[1], B[2], B[3], B[4], B[5]]
            prepB = [B[6], B[7]]
            w2B = [B[0], B[1], B[2], B[3]]
            st3B = [B[4], B[5], B[6], B[7]]
            bc3B_of = {0: [B[4], B[5]], 1: [B[6], B[7]]}
            tr3B_of = {0: [B[0], B[1]], 1: [B[2], B[3]]}

            prep_pe = []

            def a_slice(hc):
                par = hc % 2
                buf = aSlot[par]
                w = list(prep_b_done) + [(s_ld, cbE2_ld)] if hc < 2 else \
                    [(s_a_st[par], s_a_st[par].n)]
                em.dma("sync", buf[:, :], w2sb[:, hc * 128:(hc + 1) * 128], s_a_ld[par],
                       waits=tuple(w), transpose=True)
                ld = s_a_ld[par].n
                pb = prepB[par]
                for q4 in range(4):
                    pb.mms([lambda e, o_=pb.ap, l_=cbE2[:, 64:192],
                            r_=buf[:, q4 * 512:(q4 + 1) * 512]:
                            e.matmul(o_[:, :], l_, r_, start=True, stop=True)],
                           extra_waits=((s_a_ld[par], ld),) if q4 == 0 else ())
                    pb.ev("scalar", lambda e, d=buf[:, q4 * 512:(q4 + 1) * 512],
                          s=pb.ap: e.activation(d, s[:, :], AF.Copy, scale=HSCALE))
                em.dma("scalar" if par == 0 else "gpsimd",
                       w2p.rearrange("(dc p) x -> p dc x", p=128)
                       [:, :, hc * 128:(hc + 1) * 128],
                       buf.rearrange("p (dc d) -> p dc d", d=128),
                       s_a_st[par], waits=((pb.s_ev, pb.s_ev.n),))

            for hc in range(FHC):
                prep_pe.append(lambda hc=hc: a_slice(hc))

            def pump3(n):
                for _ in range(n):
                    if prep_pe:
                        prep_pe.pop(0)()

            hid_free = ()
            yg_free = ()
            res2_free = ()
            w1_free = {0: (), 1: ()}
            w3_free = ()
            w2_free = {0: (), 1: ()}
            prev_hid = ()
            last_mult = {0: (), 1: ()}

            def hb_caps():
                return tuple((b_.s_ev, b_.s_ev.n) for b_ in hbB)

            for gi, (t0g, G) in enumerate(GROUPS):
                ttiles = [(0, 512), (512, G - 512)] if G > 512 else [(0, 512)]
                em.dma("sync", y_grp[:, 0:NKC * G].rearrange("p (c t) -> p c t", c=NKC),
                       y_bfd.rearrange("(c p) t -> p c t", p=128)[:, :, t0g:t0g + G],
                       s_yg, waits=((s_y, y_stores),) + yg_free if gi == 0 else yg_free)
                yg_done = s_yg.n

                first_hid = True
                for opan in range(FH // 512):
                    w1buf = w1pan if opan % 2 == 0 else w1pre
                    w = list(w1_free[opan % 2])
                    if gi == 0 and opan < 2:
                        w.extend(prep_w13_done)
                    if gi > 0 and opan == 0:
                        w.extend(prev_hid)
                    sem1 = s_w1p[opan % 2]
                    em.dma("sync", w1buf.rearrange("p (f c) -> p f c", c=D),
                           w1t.rearrange("(f p) c -> p f c", p=128)
                           [:, opan * 4:(opan + 1) * 4, :], sem1, waits=tuple(w))
                    w1_done = (sem1, sem1.n)
                    w3w = list(w3_free)
                    if gi == 0 and opan < 2:
                        w3w.extend(prep_w13_done)
                    em.dma("sync", w3pan.rearrange("p (f c) -> p f c", c=D),
                           w3t.rearrange("(f p) c -> p f c", p=128)
                           [:, opan * 4:(opan + 1) * 4, :], s_w3, waits=tuple(w3w))
                    w3_done = (s_w3, s_w3.n)
                    gcaps, lcaps = [], []
                    for oc4 in range(4):
                        oc = opan * 4 + oc4
                        for ti, (tt, tw) in enumerate(ttiles):
                            gb = hbB[(oc * 4 + 2 * ti) % 6]
                            lb = hbB[(oc * 4 + 2 * ti + 1) % 6]
                            makes = []
                            for kc in range(NKC):
                                makes.append(lambda e, o_=gb.ap,
                                             l_=w1buf[:, oc4 * D + kc * 128:
                                                      oc4 * D + kc * 128 + 128],
                                             r_=y_grp[:, kc * G + tt:kc * G + tt + tw],
                                             st_=(kc == 0), sp_=(kc == NKC - 1), tw=tw:
                                             e.matmul(o_[:, 0:tw], l_, r_,
                                                      start=st_, stop=sp_))
                            exw = [w1_done]
                            if oc4 == 0 and ti == 0:
                                exw.append((s_yg, yg_done))
                            gb.mms(makes, extra_waits=tuple(exw))
                            makes = []
                            for kc in range(NKC):
                                makes.append(lambda e, o_=lb.ap,
                                             l_=w3pan[:, oc4 * D + kc * 128:
                                                      oc4 * D + kc * 128 + 128],
                                             r_=y_grp[:, kc * G + tt:kc * G + tt + tw],
                                             st_=(kc == 0), sp_=(kc == NKC - 1), tw=tw:
                                             e.matmul(o_[:, 0:tw], l_, r_,
                                                      start=st_, stop=sp_))
                            lb.mms(makes, extra_waits=(w3_done,))
                            gcaps.append((gb.s_mm, gb.s_mm.n))
                            lcaps.append((lb.s_mm, lb.s_mm.n))
                            sl = silu_scr[(oc * len(ttiles) + ti) % 2]
                            gb.ev("scalar", lambda e, d=sl[:, 0:tw], s=gb.ap, tw=tw:
                                  e.activation(d, s[:, 0:tw], AF.Silu),
                                  extra_waits=last_mult[(oc * len(ttiles) + ti) % 2])
                            sil_ready = (gb.s_ev, gb.s_ev.n)
                            exw = (sil_ready,)
                            if first_hid:
                                exw = exw + hid_free
                                first_hid = False
                            lb.ev("vector", lambda e,
                                  d=hidden[:, oc * G + tt:oc * G + tt + tw],
                                  a=sl[:, 0:tw], s=lb.ap, tw=tw:
                                  e.tensor_tensor(d, a, s[:, 0:tw], ALU.mult),
                                  extra_waits=exw)
                            last_mult[(oc * len(ttiles) + ti) % 2] = \
                                ((lb.s_ev, lb.s_ev.n),)
                    w1_free[opan % 2] = tuple(gcaps[-8:])
                    w3_free = tuple(lcaps[-8:])
                    if gi == 0:
                        pump3(12)
                hid_done = hb_caps()
                prev_hid = hid_done
                if gi == 0:
                    pump3(len(prep_pe))

                first_r2 = True
                stcapss = []
                r2caps = {}
                rows3_gate = globals().get('_nope', ())
                for dc in range(NKC):
                    wbuf = w2pan[dc % 2]
                    w = list(w2_free[dc % 2])
                    if gi == 0 and dc < 2:
                        w.extend([(s_a_st[0], s_a_st[0].n), (s_a_st[1], s_a_st[1].n)])
                    if dc < 2:
                        w.extend(hid_done)
                    semw2 = s_w2p[dc % 2]
                    em.dma("sync", wbuf[:, :],
                           w2p[dc * 128:(dc + 1) * 128, :], semw2, waits=tuple(w))
                    w2_ready = (semw2, semw2.n)
                    mcaps = []
                    for ti, (tt, tw) in enumerate(ttiles):
                        mb = w2B[(dc * 2 + ti) % 4]
                        makes = []
                        for hc in range(FHC):
                            makes.append(lambda e, o_=mb.ap,
                                         l_=wbuf[:, hc * 128:(hc + 1) * 128],
                                         r_=hidden[:, hc * G + tt:hc * G + tt + tw],
                                         st_=(hc == 0), sp_=(hc == FHC - 1), tw=tw:
                                         e.matmul(o_[:, 0:tw], l_, r_,
                                                  start=st_, stop=sp_))
                        exw = [w2_ready]
                        if ti == 0 and dc == 0:
                            exw.extend(hid_done)
                        mb.mms(makes, extra_waits=tuple(exw))
                        mcaps.append((mb.s_mm, mb.s_mm.n))
                        exw2 = ()
                        if first_r2:
                            exw2 = res2_free
                            first_r2 = False
                        mb.ev("vector", lambda e,
                              d=u_res2[:, dc * G + tt:dc * G + tt + tw],
                              s=mb.ap, a=y_grp[:, dc * G + tt:dc * G + tt + tw], tw=tw:
                              e.tensor_tensor(d, s[:, 0:tw], a, ALU.add),
                              extra_waits=exw2)
                        r2_ready = (mb.s_ev, mb.s_ev.n)
                        r2caps[(ti, dc)] = r2_ready
                        em.op("scalar", lambda e, d=sq3[:, 0:tw],
                              s=u_res2[:, dc * G + tt:dc * G + tt + tw], tw=tw:
                              e.activation(d, s, AF.Square),
                              waits=(r2_ready,
                                     (st3B[2 * ti + 1].s_mm, st3B[2 * ti + 1].s_mm.n)),
                              incs=((s_misc, 1),))
                        sq_ready = s_misc.n
                        st3B[2 * ti].mms([lambda e, o_=st3B[2 * ti].ap[0:1, :],
                                          l_=ones_bf[:, 0:1],
                                          r_=u_res2[:, dc * G + tt:dc * G + tt + tw],
                                          st_=(dc == 0), sp_=(dc == NKC - 1), tw=tw:
                                          e.matmul(o_[0:1, 0:tw], l_, r_,
                                                   start=st_, stop=sp_)],
                                         extra_waits=(r2_ready,))
                        st3B[2 * ti + 1].mms([lambda e, o_=st3B[2 * ti + 1].ap[0:1, :],
                                              l_=ones_bf[:, 0:1], r_=sq3[:, 0:tw],
                                              st_=(dc == 0), sp_=(dc == NKC - 1), tw=tw:
                                              e.matmul(o_[0:1, 0:tw], l_, r_,
                                                       start=st_, stop=sp_)],
                                             extra_waits=((s_misc, sq_ready),))
                    w2_free[dc % 2] = tuple(mcaps)
                    stcapss = mcaps

                res2_caps = []
                for ti, (tt, tw) in enumerate(ttiles):
                    bc3B = bc3B_of[ti]
                    tr3B = tr3B_of[ti]
                    Rm = rows3[0:1, 0:tw]
                    Rv = rows3[0:1, 512:512 + tw]
                    Ra = rows3[0:1, 1024:1024 + tw]
                    Rb = rows3[0:1, 1536:1536 + tw]
                    st3B[2 * ti].ev("vector", lambda e, d=Rm, s=st3B[2 * ti].ap, tw=tw:
                                    e.tensor_scalar_mul(d, s[0:1, 0:tw], 1.0 / D),
                                    extra_waits=((s_misc, s_misc.n),) + rows3_gate)
                    st3B[2 * ti + 1].ev("vector", lambda e, d=Rv,
                                        s=st3B[2 * ti + 1].ap, tw=tw:
                                        e.tensor_scalar_mul(d, s[0:1, 0:tw], 1.0 / D))
                    em.op("vector", lambda e, d=Ra, m=Rm: e.tensor_tensor(d, m, m, ALU.mult),
                          waits=((st3B[2 * ti].s_ev, st3B[2 * ti].s_ev.n),),
                          incs=((s_misc, 1),))
                    em.op("vector", lambda e, d=Ra: e.tensor_scalar_add(d, d, -EPS),
                          waits=((s_misc, s_misc.n),), incs=((s_misc, 1),))
                    em.op("vector", lambda e, d=Rv, a=Ra:
                          e.tensor_tensor(d, d, a, ALU.subtract),
                          waits=((s_misc, s_misc.n),
                                 (st3B[2 * ti + 1].s_ev, st3B[2 * ti + 1].s_ev.n)),
                          incs=((s_misc, 1),))
                    var_r = s_misc.n
                    em.op("scalar", lambda e, d=Ra, v=Rv:
                          e.activation(d, v, AF.Sqrt),
                          waits=((s_misc, var_r),), incs=((s_misc, 1),))
                    sq_r = s_misc.n
                    em.op("vector", lambda e, d=Ra: e.reciprocal(d, d),
                          waits=((s_misc, sq_r),), incs=((s_misc, 1),))
                    em.op("vector", lambda e, d=Rb, m=Rm, a=Ra:
                          e.scalar_tensor_tensor(d, m, -1.0, a, ALU.mult, ALU.mult),
                          waits=((s_misc, s_misc.n),), incs=((s_misc, 1),))
                    ab_r = s_misc.n
                    if debug:
                        em.dma("gpsimd", dbg["dbg_rows3"][gi * 2 + ti:gi * 2 + ti + 1, :],
                               rows3[0:1, 0:2048], s_dbg,
                               waits=((s_misc, ab_r), (s_dbg, s_dbg.n)))
                        rows3_dbg = ((s_dbg, s_dbg.n),)
                    else:
                        rows3_dbg = ()
                    rows3_gate = rows3_dbg
                    bc3B[0].mms([lambda e, o_=bc3B[0].ap, l_=ONES_ROW_F, r_=Ra, tw=tw:
                                 e.matmul(o_[:, 0:tw], l_, r_, start=True, stop=True)],
                                extra_waits=((s_misc, ab_r),))
                    bc3B[1].mms([lambda e, o_=bc3B[1].ap, l_=ONES_ROW_F, r_=Rb, tw=tw:
                                 e.matmul(o_[:, 0:tw], l_, r_, start=True, stop=True)])
                    ntc = tw // 128
                    u_res2_v = u_res2[:, 0:NKC * G].rearrange("p (dc t) -> p dc t", t=G)
                    for tc in range(ntc):
                        for q in range(4):
                            par = (tc * 4 + q) % 2
                            scr = apply2[par]
                            w = [(s_misc, s_misc.n)]
                            w.append(r2caps[(ti, NKC - 1)])
                            if tc == 0 and q == 0:
                                w += [(bc3B[0].s_mm, bc3B[0].s_mm.n),
                                      (bc3B[1].s_mm, bc3B[1].s_mm.n)]
                            ptr = tr3B[(tc * 4 + q) % 2]
                            w.append((ptr.s_mm, ptr.s_mm.n))
                            last_q = (tc == ntc - 1 and q == 3)
                            em.op("vector", lambda e,
                                  d=scr.rearrange("p (a b) -> p a b", a=4),
                                  r=u_res2_v[:, 4 * q:4 * q + 4,
                                             tt + tc * 128:tt + tc * 128 + 128],
                                  a=bc3B[0].ap[:, tc * 128:(tc + 1) * 128]
                                  .unsqueeze(1).broadcast_to([128, 4, 128]):
                                  e.tensor_tensor(d, r, a, ALU.mult),
                                  waits=tuple(w),
                                  incs=((bc3B[0].s_ev, 1),) if last_q
                                  else ((s_misc, 1),))
                            mult_r = ((bc3B[0].s_ev, bc3B[0].s_ev.n) if last_q
                                      else (s_misc, s_misc.n))
                            em.op("vector", lambda e,
                                  d=scr.rearrange("p (a b) -> p a b", a=4),
                                  b=bc3B[1].ap[:, tc * 128:(tc + 1) * 128]
                                  .unsqueeze(1).broadcast_to([128, 4, 128]):
                                  e.tensor_tensor(d, d, b, ALU.add),
                                  waits=(mult_r,),
                                  incs=((bc3B[1].s_ev, 1),) if last_q
                                  else ((s_misc, 1),))
                            ap_r = ((bc3B[1].s_ev, bc3B[1].s_ev.n) if last_q
                                    else (s_misc, s_misc.n))
                            trb = tr3B[(tc * 4 + q) % 2]
                            makes = []
                            for i4 in range(4):
                                makes.append(lambda e,
                                             o_=trb.ap[:, i4 * 128:(i4 + 1) * 128],
                                             i_=scr[:, i4 * 128:(i4 + 1) * 128],
                                             id_=ID_F: e.transpose(o_, i_, id_))
                            trb.mms(makes, extra_waits=(ap_r,))
                            exw = ((s_out, s_out.n),) if q == 0 else ()
                            trb.ev("vector", lambda e,
                                   d=stag[:, q * 512:(q + 1) * 512],
                                   s=trb.ap: e.tensor_copy(d, s[:, 0:512]),
                                   extra_waits=exw)
                        stg_r = ((tr3B[0].s_ev, tr3B[0].s_ev.n),
                                 (tr3B[1].s_ev, tr3B[1].s_ev.n))
                        trow = t0g + tt + tc * 128
                        em.dma("gpsimd", out[trow:trow + 128, :], stag[:, :], s_out,
                               waits=stg_r)

                    res2_caps += [(st3B[2 * ti].s_mm, st3B[2 * ti].s_mm.n),
                                  (st3B[2 * ti + 1].s_mm, st3B[2 * ti + 1].s_mm.n)]

                hid_free = tuple(stcapss)
                yg_free = tuple(stcapss) + ((w2B[0].s_ev, w2B[0].s_ev.n),
                                            (w2B[1].s_ev, w2B[1].s_ev.n),
                                            (w2B[2].s_ev, w2B[2].s_ev.n),
                                            (w2B[3].s_ev, w2B[3].s_ev.n))
                res2_free = tuple(res2_caps) + ((s_misc, s_misc.n),)
                w13_safe = ((s_out, s_out.n), (s_misc, s_misc.n)) + \
                    tuple((bk.s_mm, bk.s_mm.n) for bk in B)
                if debug:
                    w13_safe = w13_safe + ((s_dbg, s_dbg.n),)
                w1_free = {0: w13_safe, 1: w13_safe}
                w3_free = w13_safe

            if debug:
                em.dma("gpsimd", dbg["dbg_w2p"], w2p, s_dbg,
                       waits=((s_a_st[0], s_a_st[0].n), (s_a_st[1], s_a_st[1].n),
                              (s_dbg, s_dbg.n)))

            em.wait("gpsimd", s_out, s_out.n)
            if debug:
                em.wait("gpsimd", s_dbg, s_dbg.n)

        block = st.enter_context(nc.Block())
        em.flush(block)

    return nc


def _shard_inputs(L0_attn_x, L0_attn_w_qkv, L0_attn_b_qkv, L0_attn_w_out,
                  L0_attn_b_out, L0_ffn_W1, L0_ffn_W2, L0_ffn_W3, n_cores=8):
    x = np.ascontiguousarray(np.asarray(L0_attn_x, dtype=np.float32))
    cb, cfv, csins = host_consts()
    shared = {
        "w_qkv": np.ascontiguousarray(L0_attn_w_qkv, dtype=np.float32),
        "b_qkv": np.ascontiguousarray(L0_attn_b_qkv, dtype=np.float32),
        "w_out": np.ascontiguousarray(L0_attn_w_out, dtype=np.float32),
        "b_out": np.ascontiguousarray(L0_attn_b_out, dtype=np.float32),
        "W1": np.ascontiguousarray(L0_ffn_W1, dtype=np.float32),
        "W2": np.ascontiguousarray(L0_ffn_W2, dtype=np.float32),
        "W3": np.ascontiguousarray(L0_ffn_W3, dtype=np.float32),
        "cbE": cb, "cf": cfv,
    }
    in_maps = []
    for r in range(n_cores):
        b, t0 = r // 4, 2048 * (r % 4)
        m = dict(shared)
        m["x"] = np.ascontiguousarray(x[b, t0:t0 + 2048, :])
        m["csin"] = csins[r]
        in_maps.append(m)
    return in_maps


def kernel(L0_attn_x, L0_attn_w_qkv, L0_attn_b_qkv, L0_attn_w_out, L0_attn_b_out,
           L0_ffn_W1, L0_ffn_W2, L0_ffn_W3):
    nc = build([[0, 1, 2, 3], [4, 5, 6, 7]])
    in_maps = _shard_inputs(L0_attn_x, L0_attn_w_qkv, L0_attn_b_qkv, L0_attn_w_out,
                            L0_attn_b_out, L0_ffn_W1, L0_ffn_W2, L0_ffn_W3)
    res = run_bass_kernel_spmd(nc, in_maps, core_ids=list(range(8)))
    outa = np.empty((2, 8192, 2048), dtype=np.float32)
    for r in range(8):
        b, t0 = r // 4, 2048 * (r % 4)
        outa[b, t0:t0 + 2048, :] = res.results[r]["out"]
    return outa
